# version 11
# baseline (speedup 1.0000x reference)
import numpy as np
import ml_dtypes
from contextlib import ExitStack

import concourse.bass as bass
import concourse.mybir as mybir
from concourse.bass_utils import run_bass_kernel_spmd

F32 = mybir.dt.float32
BF16 = mybir.dt.bfloat16
AF = mybir.ActivationFunctionType
ALU = mybir.AluOpType
AX = mybir.AxisListType

T = 4096
D = 1024
NCORES = 8
NCTX = 4096
DCTX = 768
SCALE0 = 1.5 * (128.0 ** -0.5)
EXPB = -30.0
LN_EPS = 1e-5
BF = ml_dtypes.bfloat16


class Sem:
    __slots__ = ("name", "val", "h")

    def __init__(self, name):
        self.name = name
        self.val = 0
        self.h = None

    def w(self):
        return (self, self.val)


class Prog:
    ENGS = ("tensor", "vector", "scalar", "gpsimd", "sync")

    def __init__(self):
        self.sems = []
        self.ops = {e: [] for e in self.ENGS}
        self.n = 0
        self.pending = {}
        self.sV = self.sem("sV")
        self.sS = self.sem("sS")
        self.sG = self.sem("sG")
        self.auto = {"vector": self.sV, "scalar": self.sS, "gpsimd": self.sG}

    def sem(self, name):
        s = Sem(name)
        self.sems.append(s)
        return s

    def op(self, eng, fn, waits=(), inc="auto", amt=1):
        if inc == "auto":
            inc = self.auto.get(eng)
        a = self.auto.get(eng)
        waits = list(waits) + ([a.w()] if a is not None else []) + self.pending.pop(eng, [])
        w = {}
        for (s, v) in waits:
            if v > 0:
                w[s] = max(w.get(s, 0), v)
        w = list(w.items())
        if inc is not None:
            inc.val += amt
        self.ops[eng].append((fn, w, inc, amt))
        self.n += 1


class Bank:

    def __init__(self, P, name):
        self.P = P
        self.prod = P.sem(name + "_p")
        self.reads = []
        self.h = None

    def fill_waits(self):
        w = self.reads
        self.reads = []
        return w

    def read_waits(self):
        return [(self.prod, self.prod.val)]

    def mark_read(self):
        raise NotImplementedError

    def note(self, sem):
        self.reads.append((sem, sem.val))


def mm_group(P, bank, emits, waits=()):
    w = bank.fill_waits() + list(waits)
    n = len(emits)
    for i, fn in enumerate(emits):
        P.op("tensor", fn, waits=w if i == 0 else (),
             inc=bank.prod if i == n - 1 else None, amt=1)


def dma(P, q, out_ap, in_ap, sem, waits=(), **kw):
    P.op(q, lambda e, o=out_ap, i=in_ap, kw=kw: e.dma_start(out=o, in_=i, **kw),
         waits=waits, inc=sem, amt=16)


def rearr_w(w_ap, p=128):
    return w_ap.rearrange("(k p) o -> p k o", p=p)


class Ctx:
    pass


def register_precast(C, pieces):
    P = C.P
    C.pc_list = list(pieces)
    C.pc_i = 0
    C.pc_stage = None
    C.pc_sem_c = [P.sem("pc_c0"), P.sem("pc_c1")]
    C.pc_sem_s = [P.sem("pc_s0"), P.sem("pc_s1")]
    C.pc_entry = []


def drip_precast(C, n):
    P = C.P
    if not hasattr(C, "pc_list"):
        return
    if C.pc_stage is None:
        C.pc_stage = [C.stage[:, 1024:2048], C.sm16[:, 256:1280]]
    while n > 0 and C.pc_i < len(C.pc_list):
        i = C.pc_i
        b = i % 2
        srcap, dst, kdim = C.pc_list[i]
        stg = C.pc_stage[b]
        stg_v = stg.rearrange("p (k o) -> p k o", k=kdim) if kdim > 1 else stg
        w = [C.pc_sem_s[b].w()] + (C.pc_entry if i < 2 else [])
        dma(P, "gpsimd", stg_v, srcap, C.pc_sem_c[b], waits=w)
        dma(P, "sync", dst, stg, C.pc_sem_s[b], waits=[C.pc_sem_c[b].w()])
        C.pc_i += 1
        n -= 1


def flush_precast(C):
    drip_precast(C, 1 << 30)
    if hasattr(C, "pc_list"):
        return [C.pc_sem_s[0].w(), C.pc_sem_s[1].w()]
    return []


def barrier(C, extra=()):
    P = C.P
    snap = [b.prod.w() for b in C.pb] + [b.prod.w() for b in C.tb]
    snap += [P.sV.w(), P.sS.w(), P.sG.w()] + list(extra)
    for eng in P.ENGS:
        P.pending[eng] = list(snap)


def alloc_common(nc, E, C):
    C.xbuf = E(nc.sbuf_tensor("xbuf", [128, 32768], BF16))
    C.A1 = E(nc.sbuf_tensor("A1", [128, 32768], BF16))
    C.A2 = E(nc.sbuf_tensor("A2", [128, 25600], BF16))
    C.F1 = E(nc.sbuf_tensor("F1", [128, 2048], F32))
    C.stage = E(nc.sbuf_tensor("stage", [128, 2048], BF16))
    C.fstage = E(nc.sbuf_tensor("fstage", [128, 256], F32))
    C.small = E(nc.sbuf_tensor("small", [128, 128], F32))
    C.sm16 = E(nc.sbuf_tensor("sm16", [128, 1536], BF16))
    C.eye = E(nc.sbuf_tensor("eye", [128, 128], BF16))
    C.eyef = E(nc.sbuf_tensor("eyef", [128, 128], F32))
    C.st = E(nc.sbuf_tensor("st", [1, 1536], F32))
    C.pb = [Bank(C.P, f"pb{i}") for i in range(6)]
    for i, b in enumerate(C.pb):
        b.h = E(nc.psum_tensor(f"ps{i}", [128, 512], F32))
    C.tb = [Bank(C.P, f"tb{i}") for i in range(2)]
    for i, b in enumerate(C.tb):
        b.h = E(nc.psum_tensor(f"pt{i}", [128, 1024], BF16))
    C.sV = C.P.sV
    C.sS = C.P.sS
    C.sG = C.P.sG


def emit_init(C):
    P = C.P
    io = C.fstage[:, 0:128]
    P.op("gpsimd", lambda e: e.iota(io, [[-1, 128]], base=0, channel_multiplier=1,
                                    allow_small_or_imprecise_dtypes=True))
    g0 = C.sG.w()
    P.op("vector", lambda e: e.tensor_scalar(C.eye[:, :], io, 0.0, None, ALU.is_equal),
         waits=[g0])
    eyev = C.sV.w()
    P.op("vector", lambda e: e.tensor_copy(C.eyef[:, :], C.eye[:, :]), waits=[eyev])
    P.op("vector", lambda e: e.memset(C.small[:, 0:1], EXPB))
    P.op("vector", lambda e: e.memset(C.small[:, 1:2], LN_EPS))
    P.op("vector", lambda e: e.memset(C.small[:, 2:3], 1.0))
    P.op("vector", lambda e: e.memset(C.fstage[0:1, 128:256], 1.0))
    P.op("vector", lambda e: e.memset(C.sm16[:, 0:1], 1.0))
    P.op("vector", lambda e: e.memset(C.sm16[0:1, 2:4], 1.0))
    C.expb = C.small[:, 0:1]
    C.epsv = C.small[:, 1:2]
    C.eps1 = C.small[0:1, 1:2]
    C.ones_f_col = C.small[:, 2:3]
    C.ones_f_row = C.fstage[0:1, 128:256]
    C.ones_b_col = C.sm16[:, 0:1]
    C.ones_b_1 = C.sm16[0:1, 2:3]
    C.jn_off = 8192
    C.init_done = C.sV.w()


def junction(C, y_waits, width, out_writes, banks, ybuf=None):
    P = C.P
    sV, sS = C.sV, C.sS
    stat1, stat2, bcA, bcB = banks
    if ybuf is None:
        ybuf = C.F1
    W8 = 8 * width
    ybf = C.A2[:, C.jn_off: C.jn_off + W8]
    ysq = C.A2[:, C.jn_off + W8: C.jn_off + 2 * W8]
    P.op("vector", lambda e, ybuf=ybuf: e.tensor_copy(ybf, ybuf[:, 0:W8]),
         waits=list(y_waits) + [(stat1.prod, stat1.prod.val)])
    wyb = sV.w()
    P.op("scalar", lambda e: e.activation(ysq, ybf, AF.Square),
         waits=[wyb, (stat2.prod, stat2.prod.val)])
    wys = sS.w()
    ems1 = [(lambda e, d=d: e.matmul(stat1.h[0:1, 0:width], C.ones_b_col,
                                     ybf[:, d * width:(d + 1) * width],
                                     start=(d == 0), stop=(d == 7))) for d in range(8)]
    ems2 = [(lambda e, d=d: e.matmul(stat2.h[0:1, 0:width], C.ones_b_col,
                                     ysq[:, d * width:(d + 1) * width],
                                     start=(d == 0), stop=(d == 7))) for d in range(8)]
    mm_group(P, stat1, ems1, waits=[wyb, C.init_done])
    mm_group(P, stat2, ems2, waits=[wys])
    assert width <= 256
    mu = C.st[0:1, 0:width]
    e2 = C.st[0:1, 256:256 + width]
    var = e2
    arow = C.st[0:1, 512:512 + width]
    brow = C.st[0:1, 768:768 + width]
    tmp = C.st[0:1, 1024:1024 + width]
    P.op("vector", lambda e: e.tensor_scalar(mu, stat1.h[0:1, 0:width], 1.0 / D, None, ALU.mult),
         waits=stat1.read_waits())
    stat1.note(sV)
    wmu = sV.w()
    P.op("vector", lambda e: e.tensor_scalar(e2, stat2.h[0:1, 0:width], 1.0 / D, None, ALU.mult),
         waits=stat2.read_waits())
    stat2.note(sV)
    P.op("vector", lambda e: e.tensor_tensor(tmp, mu, mu, ALU.mult), waits=[wmu])
    wtmp = sV.w()
    P.op("vector", lambda e: e.tensor_tensor(var, e2, tmp, ALU.subtract), waits=[wtmp])
    wvar = sV.w()
    P.op("scalar", lambda e: e.activation(var, var, AF.Sqrt, bias=C.eps1, scale=1.0),
         waits=[wvar])
    wsd = sS.w()
    P.op("vector", lambda e: e.reciprocal(arow, var), waits=[wsd])
    wa = sV.w()
    P.op("vector", lambda e: e.scalar_tensor_tensor(brow, mu, -1.0, arow, ALU.mult, ALU.mult),
         waits=[wa])
    wab = sV.w()
    mm_group(P, bcA, [lambda e: e.matmul(bcA.h[:, 0:width], C.ones_f_row, arow,
                                         start=True, stop=True)], waits=[wab])
    mm_group(P, bcB, [lambda e: e.matmul(bcB.h[:, 0:width], C.ones_f_row, brow,
                                         start=True, stop=True)], waits=[wab])
    rb = bcA.read_waits() + bcB.read_waits()
    for dpt in range(8):
        P.op("vector", lambda e, d=dpt, ybuf=ybuf: e.tensor_tensor(
            ybuf[:, d * width:(d + 1) * width], ybuf[:, d * width:(d + 1) * width],
            bcA.h[:, 0:width], ALU.mult),
            waits=rb if dpt == 0 else ())
    out_writes(bcA, bcB)
    bcA.note(sV)
    bcB.note(sV)


def emit_l0attn(C, xin, ctx_in, wq_d, wk_d, wv_d, wout_d, bout_d, kvK, kvV, x1hbm,
                spill_x1=True):
    nc, P = C.nc, C.P
    sV, sS, sG = C.sV, C.sS, C.sG

    stg = [C.stage[:, 0:1024], C.stage[:, 1024:2048]]
    sXin = [P.sem("l0a_xin0"), P.sem("l0a_xin1")]
    stg_free = [[], []]
    for t in range(32):
        b = t % 2
        tb = C.tb[b]
        dma(P, "gpsimd", stg[b], xin[128 * t:128 * (t + 1), :], sXin[b],
            waits=stg_free[b])
        wfill = tb.fill_waits() + [sXin[b].w(), C.init_done]
        for j in range(8):
            P.op("tensor",
                 lambda e, j=j, b=b: e.transpose(C.tb[b].h[:, 128 * j:128 * (j + 1)],
                                                 stg[b][:, 128 * j:128 * (j + 1)], C.eye[:, :]),
                 waits=wfill if j == 0 else (),
                 inc=tb.prod if j == 7 else None)
        stg_free[b] = [(tb.prod, tb.prod.val)]
        xv = C.xbuf[:, :].rearrange("p (k t) -> p k t", k=8)[:, :, 128 * t:128 * (t + 1)]
        P.op("vector", lambda e, xv=xv, tb=tb: e.tensor_copy(
            xv, tb.h[:, :].rearrange("p (k t) -> p k t", k=8)),
            waits=tb.read_waits())
        tb.note(sV)
    bout_sb = C.small[:, 16:24]
    sBout = P.sem("l0a_bout")
    dma(P, "gpsimd", bout_sb, bout_d.rearrange("(k p) -> p k", p=128), sBout,
        allow_slow_non_contiguous=True)

    wk_sb = C.A2[:, 0:6144].rearrange("p (k o) -> p k o", k=6)
    wv_sb = C.A2[:, 6144:12288].rearrange("p (k o) -> p k o", k=6)
    sWkv = P.sem("l0a_wkv")
    dma(P, "gpsimd", wk_sb, rearr_w(wk_d), sWkv)
    dma(P, "gpsimd", wv_sb, rearr_w(wv_d), sWkv)
    wkv_ld = sWkv.w()
    cst = [C.stage[:, 0:768], C.stage[:, 1024:1024 + 768]]
    sCtx = [P.sem("l0a_ctx0"), P.sem("l0a_ctx1")]
    kst = C.A2[:, 12288:13312]
    vst = C.A2[:, 13312:14336]
    ctile = C.A2[:, 14336:15104]
    kb0, kb1, vb0, vb1 = C.pb[0], C.pb[1], C.pb[2], C.pb[3]
    sKsp = P.sem("l0a_ksp")
    sVsp = P.sem("l0a_vsp")
    ctile_war = []
    for ct in range(32):
        b = ct % 2
        tb = C.tb[b]
        dma(P, "gpsimd", cst[b], ctx_in[128 * ct:128 * (ct + 1), :], sCtx[b],
            waits=stg_free[b])
        wfill = tb.fill_waits() + [sCtx[b].w()]
        for j in range(6):
            P.op("tensor",
                 lambda e, j=j, b=b: e.transpose(C.tb[b].h[:, 128 * j:128 * (j + 1)],
                                                 cst[b][:, 128 * j:128 * (j + 1)], C.eye[:, :]),
                 waits=wfill if j == 0 else (),
                 inc=tb.prod if j == 5 else None)
        stg_free[b] = [(tb.prod, tb.prod.val)]
        P.op("vector", lambda e, tb=tb: e.tensor_copy(ctile[:, :], tb.h[:, 0:768]),
             waits=tb.read_waits() + ctile_war)
        tb.note(sV)
        vC = sV.w()
        for kb, hbase in ((kb0, 0), (kb1, 4)):
            ems = []
            for hh in range(4):
                h = hbase + hh
                for dpt in range(6):
                    ems.append(lambda e, h=h, hh=hh, dpt=dpt, kb=kb: e.matmul(
                        kb.h[:, hh * 128:(hh + 1) * 128],
                        wk_sb[:, dpt, h * 128:(h + 1) * 128],
                        ctile[:, 128 * dpt:128 * (dpt + 1)],
                        start=(hh == 0 and dpt == 0), stop=(hh == 3 and dpt == 5)))
            mm_group(P, kb, ems, waits=[wkv_ld, vC])
        P.op("vector", lambda e: e.tensor_copy(kst[:, 0:512], kb0.h[:, :]),
             waits=kb0.read_waits() + [sKsp.w()])
        kb0.note(sV)
        P.op("vector", lambda e: e.tensor_copy(kst[:, 512:1024], kb1.h[:, :]),
             waits=kb1.read_waits())
        kb1.note(sV)
        kvC = sV.w()
        kout = kvK[:, :].rearrange("p (h t) -> p h t", h=8)[:, :, 128 * ct:128 * (ct + 1)]
        dma(P, "sync", kout, kst.rearrange("p (h t) -> p h t", h=8), sKsp, waits=[kvC])
        for half, vb in ((0, vb0), (1, vb1)):
            ems = [(lambda e, dpt=dpt, half=half, vb=vb: e.matmul(
                vb.h[:, :], ctile[:, 128 * dpt:128 * (dpt + 1)],
                wv_sb[:, dpt, half * 512:(half + 1) * 512],
                start=(dpt == 0), stop=(dpt == 5))) for dpt in range(6)]
            mm_group(P, vb, ems, waits=[wkv_ld, vC])
            P.op("vector", lambda e, vb=vb, half=half: e.tensor_copy(
                vst[:, half * 512:(half + 1) * 512], vb.h[:, :]),
                waits=vb.read_waits() + ([sVsp.w()] if half == 0 else []))
            vb.note(sV)
        ctile_war = [(vb1.prod, vb1.prod.val)]
        dma(P, "sync", kvV[128 * ct:128 * (ct + 1), :], vst, sVsp, waits=[sV.w()])

    kKsb = C.A2[:, 0:4096]
    kVsb = C.A2[:, 4096:8192]
    ptb = C.A2[:, 8192:24576]
    qch = C.A2[:, 24576:25600]
    sKV = P.sem("l0a_kv")
    sWq = P.sem("l0a_wq")
    QB = C.pb[0]
    SBKS = [C.pb[1], C.pb[2], C.pb[3], C.pb[4]]
    OB = C.pb[5]
    DB = C.tb[0]
    RB = C.tb[1]
    DBv = C.tb[0].h[:, :].bitcast(F32)
    RBv = C.tb[1].h[:, :].bitcast(F32)
    rcp = C.st[0:1, 1024:1536]
    kv_free = []
    wq_free = []
    q_free = [[], []]
    prev_half = []
    prev_full = []
    prev_tree = []
    spills = [sKsp.w(), sVsp.w()]
    C.pc_entry = list(stg_free[1])
    for h in range(8):
        dma(P, "sync", kKsb, kvK[:, 4096 * h:4096 * (h + 1)], sKV,
            waits=kv_free + (spills if h == 0 else []))
        dma(P, "sync", kVsb.rearrange("p (k d) -> p k d", k=32),
            kvV[:, 128 * h:128 * (h + 1)].rearrange("(k p) d -> p k d", p=128), sKV)
        ld_k = sKV.w()
        wqslice = C.stage[:, 0:1024].rearrange("p (k o) -> p k o", k=8)
        dma(P, "gpsimd", wqslice, wq_d[:, 128 * h:128 * (h + 1)].rearrange("(k p) o -> p k o", p=128),
            sWq, waits=wq_free + (stg_free[0] if h == 0 else []))
        ld_q = sWq.w()
        drip_precast(C, 8)
        for qc in range(8):
            qb = qc % 2
            ems = [(lambda e, dpt=dpt, qc=qc, ws=wqslice: e.matmul(
                QB.h[:, :], ws[:, dpt, :],
                C.xbuf[:, 4096 * dpt + 512 * qc: 4096 * dpt + 512 * (qc + 1)],
                start=(dpt == 0), stop=(dpt == 7))) for dpt in range(8)]
            mm_group(P, QB, ems, waits=[ld_q] + q_free[qb] + ([ld_k] if qc == 0 else []))
            P.op("vector", lambda e, qb=qb: e.tensor_copy(qch[:, 512 * qb:512 * (qb + 1)], QB.h[:, :]),
                 waits=QB.read_waits())
            QB.note(sV)
            qdone = sV.w()
            ob_w = OB.fill_waits()
            edone_at = {}

            def emit_O(kt):
                P.op("tensor", lambda e, kt=kt: e.matmul(
                    OB.h[:, :], kVsb[:, 128 * kt:128 * (kt + 1)], ptb[:, 512 * kt:512 * (kt + 1)],
                    start=(kt == 0), stop=(kt == 31)),
                    waits=[edone_at[kt]] + (ob_w if kt == 0 else []),
                    inc=OB.prod if (kt == 15 or kt == 31) else None)

            for kt in range(32):
                SB = SBKS[kt % 4]
                P.op("tensor", lambda e, kt=kt, SB=SB, qb=qb: e.matmul(
                    SB.h[:, :], kKsb[:, 128 * kt:128 * (kt + 1)], qch[:, 512 * qb:512 * (qb + 1)],
                    start=True, stop=True),
                    waits=SB.fill_waits() + ([qdone] if kt <= 3 else []),
                    inc=SB.prod)
                war = []
                if kt == 0:
                    war = prev_tree + prev_half
                elif kt == 16:
                    war = prev_full
                P.op("scalar", lambda e, kt=kt, SB=SB: e.activation(
                    ptb[:, 512 * kt:512 * (kt + 1)], SB.h[:, :], AF.Exp, bias=C.expb, scale=SCALE0),
                    waits=SB.read_waits() + war)
                SB.note(sS)
                e_done = sS.w()
                edone_at[kt] = e_done
                if kt >= 4:
                    emit_O(kt - 4)
                if kt == 1:
                    P.op("vector", lambda e: e.tensor_tensor(
                        C.F1[:, 0:512], ptb[:, 0:512], ptb[:, 512:1024], ALU.add),
                        waits=[e_done])
                elif kt > 1:
                    P.op("vector", lambda e, kt=kt: e.tensor_tensor(
                        C.F1[:, 0:512], C.F1[:, 0:512], ptb[:, 512 * kt:512 * (kt + 1)], ALU.add),
                        waits=[e_done])
            for kt in range(28, 32):
                emit_O(kt)
            prev_half = [(OB.prod, OB.prod.val - 1)]
            prev_full = [(OB.prod, OB.prod.val)]
            prev_tree = [sV.w()]
            treedone = sV.w()
            P.op("tensor", lambda e: e.matmul(DBv[0:1, :], C.ones_f_col, C.F1[:, 0:512],
                                              start=True, stop=True),
                 waits=DB.fill_waits() + [treedone], inc=DB.prod)
            P.op("vector", lambda e: e.reciprocal(rcp, DBv[0:1, :]), waits=DB.read_waits())
            DB.note(sV)
            rcpdone = sV.w()
            P.op("tensor", lambda e: e.matmul(RBv, C.ones_f_row, rcp, start=True, stop=True),
                 waits=RB.fill_waits() + [rcpdone], inc=RB.prod)
            P.op("vector", lambda e: e.tensor_copy(C.F1[:, 512:1024], RBv), waits=RB.read_waits())
            RB.note(sV)
            rbdone = sV.w()
            P.op("vector", lambda e, h=h, qc=qc: e.tensor_tensor(
                C.A1[:, 4096 * h + 512 * qc: 4096 * h + 512 * (qc + 1)],
                OB.h[:, :], C.F1[:, 512:1024], ALU.mult),
                waits=OB.read_waits() + [rbdone])
            OB.note(sV)
            q_free[qb] = [(OB.prod, OB.prod.val)]
        kv_free = [(OB.prod, OB.prod.val)]
        wq_free = [(QB.prod, QB.prod.val)]

    wout_sb = C.A2[:, 0:8192].rearrange("p (k o) -> p k o", k=8)
    sWo = P.sem("l0a_wo")
    dma(P, "gpsimd", wout_sb, rearr_w(wout_d), sWo, waits=kv_free)
    wout_ld = sWo.w()
    bout_w = sBout.w()
    zb_banks = [C.pb[0], C.pb[1]]
    W = 256
    sSp1 = P.sem("l0a_sp1")
    ybufs = [C.F1, C.A2[:, 12288:16384].bitcast(F32)]
    pend = None
    for ch in range(16):
        yb = ybufs[ch % 2]
        for dpt in range(8):
            zb = zb_banks[dpt % 2]
            ems = [(lambda e, ipt=ipt, dpt=dpt, ch=ch, zb=zb: e.matmul(
                zb.h[:, 0:W], wout_sb[:, ipt, 128 * dpt:128 * (dpt + 1)],
                C.A1[:, 4096 * ipt + W * ch: 4096 * ipt + W * (ch + 1)],
                start=(ipt == 0), stop=(ipt == 7))) for ipt in range(8)]
            mm_group(P, zb, ems, waits=[wout_ld] if (ch == 0 and dpt < 2) else [])
            P.op("vector", lambda e, dpt=dpt, ch=ch, zb=zb, yb=yb: e.scalar_tensor_tensor(
                yb[:, W * dpt:W * (dpt + 1)], zb.h[:, 0:W], bout_sb[:, dpt:dpt + 1],
                C.xbuf[:, 4096 * dpt + W * ch: 4096 * dpt + W * (ch + 1)], ALU.add, ALU.add),
                waits=zb.read_waits() + ([bout_w] if ch == 0 and dpt == 0 else []))
            zb.note(sV)
        ydone = sV.w()

        def out_writes(bcA, bcB, ch=ch, yb=yb):
            for dpt in range(8):
                P.op("vector", lambda e, dpt=dpt, ch=ch, bcB=bcB, yb=yb: e.tensor_tensor(
                    C.xbuf[:, 4096 * dpt + W * ch: 4096 * dpt + W * (ch + 1)],
                    yb[:, W * dpt:W * (dpt + 1)], bcB.h[:, 0:W], ALU.add),
                    waits=[sV.w()] if dpt == 0 else ())
            if spill_x1:
                xv = C.xbuf[:, :].rearrange("p (k t) -> p k t", k=8)[:, :, W * ch:W * (ch + 1)]
                ov = x1hbm[:, :].rearrange("p (k t) -> p k t", k=8)[:, :, W * ch:W * (ch + 1)]
                dma(P, "sync", ov, xv, sSp1, waits=[sV.w()])

        if pend is not None:
            junction(C, pend[0], W, pend[1], (C.pb[2], C.pb[3], C.pb[4], C.pb[5]), ybuf=pend[2])
        pend = ([ydone], out_writes, yb)
        drip_precast(C, 4)
    junction(C, pend[0], W, pend[1], (C.pb[2], C.pb[3], C.pb[4], C.pb[5]), ybuf=pend[2])
    C.out_dma = sSp1



def emit_l0ffn(C, x1hbm, wfc_d, lnw_d, lnb_d, wproj_d, x2hbm):
    nc, P = C.nc, C.P
    sV, sS, sG = C.sV, C.sS, C.sG
    NELEM = 4096.0 * 1024.0

    sWt = P.sem("l0f_wt")
    sLn = P.sem("l0f_ln")
    lnwv = C.small[:, 64:96]
    lnbv = C.sm16[:, 48:80]
    dma(P, "gpsimd", lnwv, lnw_d.rearrange("(k p) -> p k", p=128), sLn,
        allow_slow_non_contiguous=True)
    dma(P, "gpsimd", lnbv, lnb_d.rearrange("(k p) -> p k", p=128), sLn,
        allow_slow_non_contiguous=True)
    ln_ld = sLn.w()

    gacc = C.small[:, 24:56]
    g1 = C.small[:, 56:57]
    thr_fc = C.small[:, 57:58]
    thr_pj = C.small[:, 58:59]
    thr_fcn = C.small[:, 63:64]
    thr_pjn = C.small[:, 62:63]
    m1_bf = C.A2[:, 0:1024]
    m2_bf = C.A2[:, 1024:2048]
    qt = C.A2[:, 2048:3072]
    qus = C.A2[:, 3072:4096]
    wst = [C.A2[:, 4096:12288].bitcast(F32), C.A2[:, 12288:20480].bitcast(F32)]
    GB = C.pb[0]
    KB = C.pb[5]

    def gamma_pass(w4_view, ntiles, thr_dst, thr_neg):
        for b in range(ntiles // 4):
            ws = wst[b % 2]
            dma(P, "sync", ws.rearrange("p (k o) -> p k o", k=4), w4_view(b), sWt,
                waits=[sV.w()])
            ld = sWt.w()
            for j in range(4):
                P.op("vector", lambda e, b=b, j=j, ws=ws: e.tensor_reduce(
                    gacc[:, 4 * b + j:4 * b + j + 1], ws[:, 1024 * j:1024 * (j + 1)],
                    axis=AX.X, op=ALU.add, apply_absolute_value=True),
                    waits=[ld] if j == 0 else [])
            drip_precast(C, 4)
        P.op("vector", lambda e, n=ntiles: e.tensor_reduce(
            g1, gacc[:, 0:n], axis=AX.X, op=ALU.add))
        gsum = sV.w()
        mm_group(P, GB, [lambda e: e.matmul(GB.h[0:1, 0:1], C.ones_f_col, g1,
                                            start=True, stop=True)],
                 waits=[gsum, C.init_done])
        P.op("vector", lambda e: e.tensor_scalar(C.st[0:1, 0:1], GB.h[0:1, 0:1],
                                                 0.5 / NELEM, 0.5e-6, ALU.mult, ALU.add),
             waits=GB.read_waits())
        GB.note(sV)
        thr1 = sV.w()
        mm_group(P, GB, [lambda e: e.matmul(GB.h[:, 1:2], C.ones_f_row, C.st[0:1, 0:1],
                                            start=True, stop=True)], waits=[thr1])
        P.op("vector", lambda e: e.tensor_copy(thr_dst, GB.h[:, 1:2]),
             waits=GB.read_waits())
        GB.note(sV)
        P.op("vector", lambda e: e.tensor_scalar(thr_neg, thr_dst, -1.0, None, ALU.mult))

    def quantize_tile(wt, thr, thr_n, prev_tp, ld):
        P.op("vector", lambda e, wt=wt, thr=thr: e.tensor_scalar(
            m1_bf, wt, thr, None, ALU.is_gt), waits=[ld])
        P.op("vector", lambda e, wt=wt, thr_n=thr_n: e.tensor_scalar(
            m2_bf, wt, thr_n, None, ALU.is_lt))
        P.op("vector", lambda e: e.tensor_tensor(qt, m1_bf, m2_bf, ALU.subtract),
             waits=prev_tp)

    gamma_pass(lambda b: wfc_d[512 * b:512 * (b + 1), :].rearrange("(k p) o -> p k o", p=128),
               32, thr_fc, thr_fcn)
    prev_tp = []
    ld4 = None
    for i in range(32):
        b, j = divmod(i, 4)
        ws = wst[b % 2]
        if j == 0:
            dma(P, "sync", ws.rearrange("p (k o) -> p k o", k=4),
                wfc_d[512 * b:512 * (b + 1), :].rearrange("(k p) o -> p k o", p=128),
                sWt, waits=[sV.w()])
            ld4 = sWt.w()
            drip_precast(C, 6)
        quantize_tile(ws[:, 1024 * j:1024 * (j + 1)], thr_fc, thr_fcn, prev_tp, ld4)
        qdone = sV.w()
        tb = C.tb[i % 2]
        wfill = tb.fill_waits() + [qdone]
        for j in range(8):
            P.op("tensor", lambda e, j=j, tb=tb: e.transpose(
                tb.h[:, 128 * j:128 * (j + 1)], qt[:, 128 * j:128 * (j + 1)], C.eye[:, :]),
                waits=wfill if j == 0 else (), inc=tb.prod if j == 7 else None)
        prev_tp = [(tb.prod, tb.prod.val)]
        dst = C.A1[:, :].rearrange("p (k t) -> p k t", k=8)[:, :, 128 * i:128 * (i + 1)]
        P.op("vector", lambda e, dst=dst, tb=tb: e.tensor_copy(
            dst, tb.h[:, :].rearrange("p (k t) -> p k t", k=8)),
            waits=tb.read_waits())
        tb.note(sV)
    wfc_ready = sV.w()

    gamma_pass(lambda b: wproj_d[128 * b:128 * (b + 1), :].rearrange("p (k o) -> p k o", k=4),
               32, thr_pj, thr_pjn)
    sKmm = P.sem("l0f_kmm")
    prev_tp = []
    ld4 = None
    for i in range(32):
        j, qn = divmod(i, 4)
        ws = wst[j % 2]
        if qn == 0:
            dma(P, "sync", ws.rearrange("p (k o) -> p k o", k=4),
                wproj_d[128 * j:128 * (j + 1), :].rearrange("p (k o) -> p k o", k=4),
                sWt, waits=[sV.w()])
            ld4 = sWt.w()
            drip_precast(C, 6)
        quantize_tile(ws[:, 1024 * qn:1024 * (qn + 1)], thr_pj, thr_pjn, prev_tp, ld4)
        qdone = sV.w()
        tb = C.tb[i % 2]
        wfill = tb.fill_waits() + [qdone]
        for c in range(8):
            P.op("tensor", lambda e, c=c, tb=tb: e.transpose(
                tb.h[:, 128 * c:128 * (c + 1)], qt[:, 128 * c:128 * (c + 1)], C.eye[:, :]),
                waits=wfill if c == 0 else (), inc=tb.prod if c == 7 else None)
        prev_tp = [(tb.prod, tb.prod.val)]
        P.op("vector", lambda e, tb=tb: e.tensor_copy(qus, tb.h[:, :]),
             waits=tb.read_waits() + [(sKmm, sKmm.val)])
        tb.note(sV)
        qus_done = sV.w()
        for c in range(8):
            wk = [qus_done] if c == 0 else []
            if i == 0 and c == 0:
                wk.append(ln_ld)
            if qn == 0 and c == 0:
                wk += KB.fill_waits()
            P.op("tensor", lambda e, c=c, j=j, qn=qn: e.matmul(
                KB.h[:, 0:1], qus[:, 128 * c:128 * (c + 1)],
                lnbv[:, 8 * qn + c:8 * qn + c + 1],
                start=(qn == 0 and c == 0), stop=(qn == 3 and c == 7)),
                waits=wk, inc=sKmm if c == 7 else None)
        if qn == 3:
            P.op("vector", lambda e, j=j: e.tensor_copy(C.small[:, 24 + j:25 + j], KB.h[:, 0:1]),
                 waits=[(sKmm, sKmm.val)])
            KB.note(sV)
        for c in range(8):
            P.op("vector", lambda e, c=c, j=j, qn=qn: e.tensor_scalar(
                C.xbuf[:, (8 * qn + c) * 1024 + 128 * j: (8 * qn + c) * 1024 + 128 * (j + 1)],
                qus[:, 128 * c:128 * (c + 1)], lnwv[:, 8 * qn + c:8 * qn + c + 1], None, ALU.mult),
                waits=[ln_ld] if (i == 0 and c == 0) else ())
    konst_sb = C.small[:, 24:32]
    wq_ready = [sV.w(), (sKmm, sKmm.val)]

    hbuf = C.A2[:, 0:8192]
    hT = C.A2[:, 8192:16384]
    x2st = C.A2[:, 16384:18432]
    C.jn_off = 18432
    xt = C.A2[:, 22528:24576]
    sX1 = P.sem("l0f_x1")
    sX2 = P.sem("l0f_x2")
    musum = C.small[:, 59:61]
    sqsum = C.small[:, 61:63]
    rinv = C.small[:, 36:38]
    muv = C.small[:, 38:40]
    UPB = [C.pb[0], C.pb[1]]
    DNB = [C.pb[2], C.pb[3]]
    xt_war = []
    for pr in range(16):
        for s in range(2):
            t = 2 * pr + s
            xv = x1hbm[:, :].rearrange("p (k t) -> p k t", k=8)[:, :, 128 * t:128 * (t + 1)]
            dma(P, "sync", xt[:, 1024 * s:1024 * (s + 1)].rearrange("p (k t) -> p k t", k=8),
                xv, sX1, waits=xt_war if s == 0 else [])
        x1_ld = sX1.w()
        for s in range(2):
            for hc in range(8):
                ub = UPB[hc % 2]
                ems = [(lambda e, dpt=dpt, s=s, hc=hc, ub=ub: e.matmul(
                    ub.h[:, :], xt[:, 1024 * s + 128 * dpt: 1024 * s + 128 * (dpt + 1)],
                    C.A1[:, 4096 * dpt + 512 * hc: 4096 * dpt + 512 * (hc + 1)],
                    start=(dpt == 0), stop=(dpt == 7))) for dpt in range(8)]
                mm_group(P, ub, ems, waits=[x1_ld] + ((wq_ready + [wfc_ready]) if pr == 0 and s == 0 and hc < 2 else []))
                P.op("scalar", lambda e, s=s, hc=hc, ub=ub: e.activation(
                    hbuf[:, 4096 * s + 512 * hc: 4096 * s + 512 * (hc + 1)], ub.h[:, :], AF.Gelu),
                    waits=ub.read_waits())
                ub.note(sS)
            gelu_done = sS.w()
            P.op("vector", lambda e, s=s: e.tensor_reduce(
                musum[:, s:s + 1], hbuf[:, 4096 * s:4096 * (s + 1)], axis=AX.X, op=ALU.add),
                waits=[gelu_done])
            P.op("scalar", lambda e, s=s: e.activation(
                hT[:, 4096 * s:4096 * (s + 1)], hbuf[:, 4096 * s:4096 * (s + 1)], AF.Square,
                accum_out=sqsum[:, s:s + 1]))
            P.op("vector", lambda e, s=s: e.tensor_scalar(
                muv[:, s:s + 1], musum[:, s:s + 1], 1.0 / 4096.0, None, ALU.mult))
            P.op("vector", lambda e, s=s: e.scalar_tensor_tensor(
                rinv[:, s:s + 1], muv[:, s:s + 1], -1.0, muv[:, s:s + 1], ALU.mult, ALU.mult),
                waits=[sS.w()])
            P.op("vector", lambda e, s=s: e.scalar_tensor_tensor(
                rinv[:, s:s + 1], sqsum[:, s:s + 1], 1.0 / 4096.0, rinv[:, s:s + 1],
                ALU.mult, ALU.add))
            P.op("scalar", lambda e, s=s: e.activation(
                rinv[:, s:s + 1], rinv[:, s:s + 1], AF.Sqrt, bias=C.epsv, scale=1.0),
                waits=[sV.w()])
            P.op("vector", lambda e, s=s: e.reciprocal(rinv[:, s:s + 1], rinv[:, s:s + 1]),
                 waits=[sS.w()])
            for hf in range(2):
                P.op("vector", lambda e, s=s, hf=hf: e.tensor_scalar(
                    hbuf[:, 4096 * s + 2048 * hf: 4096 * s + 2048 * (hf + 1)],
                    hbuf[:, 4096 * s + 2048 * hf: 4096 * s + 2048 * (hf + 1)],
                    muv[:, s:s + 1], rinv[:, s:s + 1], ALU.subtract, ALU.mult))
        hn_done = sV.w()
        hT_half = None
        for cb in range(16):
            tb = C.tb[cb % 2]
            wfill = tb.fill_waits() + ([hn_done] if cb < 2 else [])
            k = 0
            for cblk in (2 * cb, 2 * cb + 1):
                for s in range(2):
                    P.op("tensor", lambda e, cblk=cblk, s=s, k=k, tb=tb: e.transpose(
                        tb.h[:, 256 * (cblk % 2) + 128 * s: 256 * (cblk % 2) + 128 * (s + 1)],
                        hbuf[:, 4096 * s + 128 * cblk: 4096 * s + 128 * (cblk + 1)], C.eye[:, :]),
                        waits=wfill if k == 0 else (),
                        inc=tb.prod if k == 3 else None)
                    k += 1
            P.op("vector", lambda e, cb=cb, tb=tb: e.tensor_copy(
                hT[:, 512 * cb:512 * (cb + 1)], tb.h[:, 0:512]),
                waits=tb.read_waits())
            tb.note(sV)
            if cb == 7:
                hT_half = sV.w()
        hT_done = sV.w()
        for dpt in range(8):
            db = DNB[dpt % 2]
            ems = [(lambda e, hpt=hpt, dpt=dpt, db=db: e.matmul(
                db.h[:, 0:256], C.xbuf[:, 1024 * hpt + 128 * dpt: 1024 * hpt + 128 * (dpt + 1)],
                hT[:, 256 * hpt:256 * (hpt + 1)],
                start=(hpt == 0), stop=(hpt == 31))) for hpt in range(32)]
            for i2, fn in enumerate(ems):
                P.op("tensor", fn,
                     waits=(db.fill_waits() + [hT_half]) if i2 == 0
                     else ([hT_done] if i2 == 16 else ()),
                     inc=db.prod if i2 == 31 else None)
            P.op("vector", lambda e, dpt=dpt, db=db: e.scalar_tensor_tensor(
                C.F1[:, 256 * dpt:256 * (dpt + 1)].rearrange("p (s t) -> p s t", s=2),
                db.h[:, 0:256].rearrange("p (s t) -> p s t", s=2), konst_sb[:, dpt:dpt + 1],
                xt.rearrange("p (s k t) -> p k s t", s=2, k=8)[:, dpt], ALU.add, ALU.add),
                waits=db.read_waits())
            db.note(sV)
        ydone = sV.w()

        def out_writes(bcA, bcB, pr=pr):
            for dpt in range(8):
                P.op("vector", lambda e, dpt=dpt, bcB=bcB: e.tensor_tensor(
                    x2st[:, 256 * dpt:256 * (dpt + 1)],
                    C.F1[:, 256 * dpt:256 * (dpt + 1)], bcB.h[:, 0:256], ALU.add),
                    waits=[sX2.w()] if dpt == 0 else ())
            ov = x2hbm[:, :].rearrange("p (k t) -> p k t", k=8)[:, :, 256 * pr:256 * (pr + 1)]
            dma(P, "sync", ov, x2st.rearrange("p (k t) -> p k t", k=8), sX2, waits=[sV.w()])

        junction(C, [ydone], 256, out_writes, (C.pb[4], C.pb[5], C.pb[2], C.pb[3]))
        drip_precast(C, 6)
        xt_war = [sV.w(), (UPB[0].prod, UPB[0].prod.val), (UPB[1].prod, UPB[1].prod.val)]
    C.out_dma = sX2



def emit_l1attn(C, x2hbm, wq_d, bq_d, wk_d, bk_d, wv_d, bv_d, wo_d, bo_d, rf_d,
                x3hbm, replica_groups, cc_in, cc_out, cc2_in, cc2_out,
                w1_d, b1_d, w2_d, b2_d, load_x2=True, scr=None):
    nc, P = C.nc, C.P
    sV, sS, sG = C.sV, C.sS, C.sG

    sX2i = P.sem("l1a_x2i")
    if load_x2:
        for q in range(4):
            dma(P, "sync", C.xbuf[:, 8192 * q:8192 * (q + 1)],
                x2hbm[:, 8192 * q:8192 * (q + 1)], sX2i)
    x2_ld = sX2i.w()
    sB = P.sem("l1a_bias")
    bkv = C.small[:, 96:104]
    bk1v = C.small[:, 104:112]
    bqv = C.small[:, 112:120]
    bq1v = C.small[:, 120:128]
    bov = C.small[:, 24:32]
    dma(P, "gpsimd", bkv, bk_d.rearrange("(k p) -> p k", p=128), sB,
        allow_slow_non_contiguous=True)
    dma(P, "gpsimd", bqv, bq_d.rearrange("(k p) -> p k", p=128), sB,
        allow_slow_non_contiguous=True)
    dma(P, "gpsimd", bov, bo_d.rearrange("(k p) -> p k", p=128), sB,
        allow_slow_non_contiguous=True)
    bvrow = C.sm16[0:1, 256:1280]
    dma(P, "gpsimd", bvrow, bv_d, sB)
    rf_all = C.sm16[:, 1280:1536]
    dma(P, "gpsimd", C.sm16[0:64, 1280:1536], rf_d, sB)
    dma(P, "gpsimd", C.sm16[64:128, 1280:1536], rf_d, sB)
    ones_b_row = C.sm16[0:1, 4:132]
    ones512_b = C.stage[0:1, 0:512]
    P.op("vector", lambda e: e.memset(ones_b_row, 1.0))
    P.op("vector", lambda e: e.memset(ones512_b, 1.0))
    bias_ld = sB.w()
    P.op("vector", lambda e: e.tensor_scalar(bk1v, bkv, 1.0, None, ALU.add), waits=[bias_ld])
    P.op("vector", lambda e: e.tensor_scalar(bq1v, bqv, 1.0, None, ALU.add))
    bias_rdy = sV.w()

    Ek = C.A2[:, 0:4096]
    kvbuf = C.A2[:, 8192:12288].bitcast(F32)
    kv_bf = C.A2[:, 12288:14336]
    qf_sb = C.A2[:, 14336:15360]
    wkp = C.A2[:, 16384:17408].rearrange("p (k o) -> p k o", k=8)
    wqp = C.A2[:, 17408:18432].rearrange("p (k o) -> p k o", k=8)
    kf_sb = C.A2[:, 18432:18944]
    kf_sb2 = C.A2[:, 23040:23552]
    C.jn_off = 18944
    sWk = P.sem("l1a_wk")
    sWq = P.sem("l1a_wq")
    KPB = [C.pb[0], C.pb[1]]

    def elu_evac(bank, bcol, b1col, out_ap):
        P.op("vector", lambda e, bank=bank, bcol=bcol: e.tensor_scalar(
            C.F1[:, 0:512], bank.h[:, :], bcol, 0.0, ALU.add, ALU.min),
            waits=bank.read_waits())
        P.op("scalar", lambda e: e.activation(C.F1[:, 512:1024], C.F1[:, 0:512], AF.Exp),
             waits=[sV.w()])
        P.op("vector", lambda e, bank=bank, b1col=b1col, out_ap=out_ap: e.scalar_tensor_tensor(
            out_ap, bank.h[:, :], b1col, C.F1[:, 512:1024], ALU.add, ALU.max),
            waits=[sS.w()])
        bank.note(sV)

    sWv = P.sem("l1a_wv")
    wvfull = C.A2[:, 8192:16384]
    for p in range(8):
        dma(P, "sync", wvfull[:, 1024 * p:1024 * (p + 1)], scr["wv"][p], sWv)
    wv_ld = sWv.w()
    wvv = wvfull.rearrange("p (pr k o) -> p k pr o", pr=8, k=8)
    KPB = [C.pb[0], C.pb[1]]
    for tt in range(32):
        for half in range(2):
            vb = KPB[half]
            ems = [(lambda e, dpt=dpt, tt=tt, half=half, vb=vb: e.matmul(
                vb.h[:, :], C.xbuf[:, 4096 * dpt + 128 * tt: 4096 * dpt + 128 * (tt + 1)],
                wvv[:, dpt, 4 * half:4 * (half + 1), :],
                start=(dpt == 0), stop=False)) for dpt in range(8)]
            ems.append(lambda e, half=half, vb=vb: e.matmul(
                vb.h[:, :], ones_b_row, bvrow[0:1, 512 * half:512 * (half + 1)],
                start=False, stop=True))
            mm_group(P, vb, ems, waits=[wv_ld, x2_ld, bias_rdy] if tt == 0 else [])
            P.op("scalar", lambda e, tt=tt, half=half, vb=vb: e.copy(
                C.A1[:, 1024 * tt + 512 * half: 1024 * tt + 512 * (half + 1)], vb.h[:, :]),
                waits=vb.read_waits())
            vb.note(sS)
    vfull_done = sS.w()

    wk_free = []
    for p in range(8):
        dma(P, "sync", C.A2[:, 16384:17408], scr["wk"][p], sWk, waits=wk_free)
        wk_ld = sWk.w()
        for qc in range(8):
            kb = KPB[qc % 2]
            ems = [(lambda e, dpt=dpt, qc=qc, kb=kb: e.matmul(
                kb.h[:, :], wkp[:, dpt, :],
                C.xbuf[:, 4096 * dpt + 512 * qc: 4096 * dpt + 512 * (qc + 1)],
                start=(dpt == 0), stop=(dpt == 7))) for dpt in range(8)]
            mm_group(P, kb, ems, waits=[wk_ld] + ([x2_ld, bias_rdy] if p == 0 and qc < 2 else []))
            elu_evac(kb, bkv[:, p:p + 1], bk1v[:, p:p + 1], Ek[:, 512 * qc:512 * (qc + 1)])
        ek_done = sV.w()
        KFB = [C.pb[0], C.pb[1]]
        kfs = [kf_sb, kf_sb2]
        for h2 in range(2):
            banks = (C.pb[2], C.pb[3]) if h2 == 0 else (C.pb[4], C.pb[5])
            kvw = banks[0].fill_waits() + banks[1].fill_waits()
            for kt2 in range(16):
                fb = KFB[kt2 % 2]
                kf = kfs[kt2 % 2]
                for u in range(2):
                    kt = 2 * kt2 + u
                    P.op("tensor", lambda e, h2=h2, kt=kt, u=u, fb=fb: e.matmul(
                        fb.h[:, 256 * u:256 * (u + 1)],
                        Ek[64 * h2:64 * (h2 + 1), 128 * kt:128 * (kt + 1)],
                        rf_all[64 * h2:64 * (h2 + 1), :], start=(u == 0), stop=(u == 1)),
                        waits=(fb.fill_waits() + ([ek_done, vfull_done] + kvw
                                                  if (h2 == 0 and kt2 <= 1) else []))
                        if u == 0 else [],
                        inc=fb.prod if u == 1 else None)
                P.op("scalar", lambda e, fb=fb, kf=kf: e.copy(kf[:, 0:512], fb.h[:, 0:512]),
                     waits=fb.read_waits())
                fb.note(sS)
                kf_done = sS.w()
                for u in range(2):
                    kt = 2 * kt2 + u
                    for fh in range(2):
                        P.op("tensor", lambda e, h2=h2, kt=kt, u=u, fh=fh, p=p,
                             kvb=banks[fh], kf=kf: e.matmul(
                            kvb.h[:, 0:64], kf[:, 256 * u + 128 * fh: 256 * u + 128 * (fh + 1)],
                            C.A1[:, 1024 * kt + 128 * p + 64 * h2: 1024 * kt + 128 * p + 64 * (h2 + 1)],
                            start=(kt == 0), stop=(kt == 31)),
                            waits=[kf_done] if (u == 0 and fh == 0) else [],
                            inc=banks[fh].prod if (kt == 31 and u == 1) else None)
            h = 2 * p + h2
            for fh in range(2):
                P.op("vector", lambda e, h=h, fh=fh, kvb=banks[fh]: e.tensor_copy(
                    kvbuf[:, 1024 * fh + 64 * h: 1024 * fh + 64 * (h + 1)],
                    kvb.h[:, 0:64]),
                    waits=banks[fh].read_waits())
                banks[fh].note(sV)
        wk_free = [(KPB[1].prod, KPB[1].prod.val), (KFB[1].prod, KFB[1].prod.val)]
    kv_done = sV.w()

    sKout = P.sem("l1a_kvout")
    sKin = P.sem("l1a_kvin")
    sCC = P.sem("l1a_cc")
    dma(P, "gpsimd", cc_in[:, :], kvbuf, sKout, waits=[kv_done])
    P.op("gpsimd", lambda e: e.collective_compute(
        "AllReduce", ALU.add, replica_groups=replica_groups,
        ins=[cc_in.ap().opt()], outs=[cc_out.ap().opt()]),
        waits=[sKout.w()], inc=sCC, amt=1)
    cc_trig = sCC.w()

    sWoB = P.sem("l1a_wob")
    wo_sb = C.A2[:, 0:8192].rearrange("p (k o) -> p k o", k=8)
    ek_vp_done = [(C.pb[5].prod, C.pb[5].prod.val), sV.w(), sKout.w()]
    if scr is not None:
        dma(P, "sync", wo_sb, scr["wo"].rearrange("p (k o) -> p k o", k=8), sWoB,
            waits=ek_vp_done)
    wq_free = []
    for p in range(8):
        if scr is not None:
            dma(P, "sync", C.A2[:, 17408:18432], scr["wq"][p], sWq, waits=wq_free)
        else:
            dma(P, "gpsimd", wqp, wq_d[:, 128 * p:128 * (p + 1)].rearrange("(k p) o -> p k o", p=128),
                sWq, waits=wq_free)
        wq_ld = sWq.w()
        for qc in range(8):
            qb = KPB[qc % 2]
            ems = [(lambda e, dpt=dpt, qc=qc, qb=qb: e.matmul(
                qb.h[:, :], wqp[:, dpt, :],
                C.xbuf[:, 4096 * dpt + 512 * qc: 4096 * dpt + 512 * (qc + 1)],
                start=(dpt == 0), stop=(dpt == 7))) for dpt in range(8)]
            mm_group(P, qb, ems, waits=[wq_ld])
            elu_evac(qb, bqv[:, p:p + 1], bq1v[:, p:p + 1],
                     C.A1[:, 4096 * p + 512 * qc: 4096 * p + 512 * (qc + 1)])
        wq_free = [(KPB[1].prod, KPB[1].prod.val)]
    eq_done = sV.w()
    if scr is None:
        dma(P, "gpsimd", wo_sb, rearr_w(wo_d), sWoB, waits=ek_vp_done)

    dma(P, "gpsimd", kvbuf, cc_out[:, :], sKin, waits=[cc_trig])
    P.op("vector", lambda e: e.tensor_copy(kv_bf, kvbuf), waits=[sKin.w()])
    kvbf_done = sV.w()

    QF = [C.pb[0], C.pb[1]]
    OB2 = [C.pb[2], C.pb[3]]
    for p in range(8):
        for h2 in range(2):
            for qc in range(8):
                for fh in range(2):
                    qfb = QF[fh]
                    mm_group(P, qfb, [lambda e, p=p, h2=h2, qc=qc, fh=fh, qfb=qfb: e.matmul(
                        qfb.h[:, :], rf_all[64 * h2:64 * (h2 + 1), 128 * fh:128 * (fh + 1)],
                        C.A1[64 * h2:64 * (h2 + 1), 4096 * p + 512 * qc: 4096 * p + 512 * (qc + 1)],
                        start=True, stop=True)],
                        waits=[eq_done] if (p == 0 and h2 == 0 and qc == 0) else [])
                    P.op("vector", lambda e, fh=fh, qfb=qfb: e.tensor_copy(
                        qf_sb[:, 512 * fh:512 * (fh + 1)], qfb.h[:, :]),
                        waits=qfb.read_waits())
                    qfb.note(sV)
                qf_done = sV.w()
                ob = OB2[qc % 2]
                h = 2 * p + h2
                ems = [(lambda e, h=h, fh=fh2, ob=ob: e.matmul(
                    ob.h[0:64, :], kv_bf[:, 1024 * fh + 64 * h: 1024 * fh + 64 * (h + 1)],
                    qf_sb[:, 512 * fh:512 * (fh + 1)],
                    start=(fh == 0), stop=(fh == 1))) for fh2 in range(2)]
                mm_group(P, ob, ems,
                         waits=[qf_done] + ([kvbf_done] if p == 0 and h2 == 0 and qc == 0 else []))
                P.op("vector", lambda e, h2=h2, p=p, qc=qc, ob=ob: e.tensor_copy(
                    C.A1[64 * h2:64 * (h2 + 1), 4096 * p + 512 * qc: 4096 * p + 512 * (qc + 1)],
                    ob.h[0:64, :]),
                    waits=ob.read_waits())
                ob.note(sV)
    out_done = sV.w()

    wo_ld = sWoB.w()
    zb_banks = [C.pb[0], C.pb[1]]
    W = 256
    sSp3 = P.sem("l1a_sp3")
    ybufs = [C.F1, C.A2[:, 8192:12288].bitcast(F32)]
    pend = None
    for ch in range(16):
        yb = ybufs[ch % 2]
        for dpt in range(8):
            zb = zb_banks[dpt % 2]
            ems = [(lambda e, ipt=ipt, dpt=dpt, ch=ch, zb=zb: e.matmul(
                zb.h[:, 0:W], wo_sb[:, ipt, 128 * dpt:128 * (dpt + 1)],
                C.A1[:, 4096 * ipt + W * ch: 4096 * ipt + W * (ch + 1)],
                start=(ipt == 0), stop=(ipt == 7))) for ipt in range(8)]
            mm_group(P, zb, ems, waits=[wo_ld, out_done] if (ch == 0 and dpt < 2) else [])
            P.op("vector", lambda e, dpt=dpt, ch=ch, zb=zb, yb=yb: e.scalar_tensor_tensor(
                yb[:, W * dpt:W * (dpt + 1)], zb.h[:, 0:W], bov[:, dpt:dpt + 1],
                C.xbuf[:, 4096 * dpt + W * ch: 4096 * dpt + W * (ch + 1)], ALU.add, ALU.add),
                waits=zb.read_waits())
            zb.note(sV)
        ydone = sV.w()

        def out_writes(bcA, bcB, ch=ch, yb=yb):
            for dpt in range(8):
                P.op("vector", lambda e, dpt=dpt, ch=ch, bcB=bcB, yb=yb: e.tensor_tensor(
                    C.xbuf[:, 4096 * dpt + W * ch: 4096 * dpt + W * (ch + 1)],
                    yb[:, W * dpt:W * (dpt + 1)], bcB.h[:, 0:W], ALU.add))
            xv = C.xbuf[:, :].rearrange("p (k t) -> p k t", k=8)[:, :, W * ch:W * (ch + 1)]
            ov = x3hbm[:, :].rearrange("p (k t) -> p k t", k=8)[:, :, W * ch:W * (ch + 1)]
            dma(P, "sync", ov, xv, sSp3, waits=[sV.w()])

        if pend is not None:
            junction(C, pend[0], W, pend[1], (C.pb[2], C.pb[3], C.pb[4], C.pb[5]), ybuf=pend[2])
        pend = ([ydone], out_writes, yb)
    junction(C, pend[0], W, pend[1], (C.pb[2], C.pb[3], C.pb[4], C.pb[5]), ybuf=pend[2])
    x3_done = sV.w()

    pooled = C.small[:, 32:40]
    for dpt in range(8):
        P.op("vector", lambda e, dpt=dpt: e.tensor_reduce(
            pooled[:, dpt:dpt + 1], C.xbuf[:, 4096 * dpt:4096 * (dpt + 1)],
            axis=AX.X, op=ALU.add), waits=[x3_done] if dpt == 0 else ())
    P.op("vector", lambda e: e.tensor_scalar(pooled, pooled, 1.0 / 8192.0, None, ALU.mult))
    pooled_done = sV.w()
    sPout = P.sem("l1a_pout")
    sPin = P.sem("l1a_pin")
    dma(P, "gpsimd", cc2_in[:, :], pooled, sPout, waits=[pooled_done])
    P.op("gpsimd", lambda e: e.collective_compute(
        "AllReduce", ALU.add, replica_groups=replica_groups,
        ins=[cc2_in.ap().opt()], outs=[cc2_out.ap().opt()]),
        waits=[sPout.w()], inc=sCC, amt=1)
    dma(P, "gpsimd", pooled, cc2_out[:, :], sPin, waits=[sCC.w()])
    p_ld = sPin.w()
    C.wvec = C.small[:, 48:56]
    emit_router(C, p_ld, pooled, w1_d, b1_d, w2_d, b2_d)
    C.out_dma = sSp3


def emit_router(C, p_ld, pooled, w1_d, b1_d, w2_d, b2_d):
    P = C.P
    sV, sS = C.sV, C.sS
    sRW = P.sem("rt_w")
    w1_sb = C.stage[:, 0:2048].rearrange("p (k o) -> p k o", k=8)
    w2_sb = C.sm16[:, 140:156].rearrange("p (k o) -> p k o", k=2)
    b1v = C.small[:, 40:42]
    b2bf = C.sm16[0:1, 160:168]
    pooled_bf = C.sm16[:, 132:140]
    h1bf = C.sm16[:, 156:158]
    dma(P, "gpsimd", w1_sb, rearr_w(w1_d), sRW)
    dma(P, "gpsimd", w2_sb, rearr_w(w2_d), sRW)
    dma(P, "gpsimd", b1v, b1_d.rearrange("(k p) -> p k", p=128), sRW,
        allow_slow_non_contiguous=True)
    dma(P, "gpsimd", b2bf, b2_d, sRW)
    rw_ld = sRW.w()
    P.op("vector", lambda e: e.tensor_copy(pooled_bf, pooled), waits=[p_ld])
    pb_done = sV.w()
    HB = C.pb[0]
    hw = HB.fill_waits()
    for hs in range(2):
        for dpt in range(8):
            P.op("tensor", lambda e, dpt=dpt, hs=hs: e.matmul(
                HB.h[:, hs:hs + 1], w1_sb[:, dpt, 128 * hs:128 * (hs + 1)],
                pooled_bf[:, dpt:dpt + 1],
                start=(hs == 0 and dpt == 0), stop=(hs == 1 and dpt == 7)),
                waits=(hw + [rw_ld, pb_done]) if hs == 0 and dpt == 0 else [],
                inc=HB.prod if (hs == 1 and dpt == 7) else None)
    for hs in range(2):
        P.op("scalar", lambda e, hs=hs: e.activation(
            h1bf[:, hs:hs + 1], HB.h[:, hs:hs + 1], AF.Tanh, bias=b1v[:, hs:hs + 1], scale=1.0),
            waits=HB.read_waits() if hs == 0 else ())
    HB.note(sS)
    h1_done = sS.w()
    LB = C.pb[1]
    ems = [(lambda e, hs=hs: e.matmul(
        LB.h[0:1, 0:8], h1bf[:, hs:hs + 1], w2_sb[:, hs, :],
        start=(hs == 0), stop=False)) for hs in range(2)]
    ems.append(lambda e: e.matmul(LB.h[0:1, 0:8], C.ones_b_1, b2bf,
                                  start=False, stop=True))
    mm_group(P, LB, ems, waits=[h1_done])
    pr = C.st[0:1, 1288:1296]
    t1 = C.st[0:1, 1296:1304]
    t2 = C.st[0:1, 1304:1312]
    m1 = C.st[0:1, 1312:1313]
    sm = C.st[0:1, 1313:1314]
    P.op("vector", lambda e: e.tensor_reduce(m1, LB.h[0:1, 0:8], axis=AX.X, op=ALU.max,
                                             negate=True), waits=LB.read_waits())
    LB.note(sV)
    P.op("scalar", lambda e: e.activation(pr, LB.h[0:1, 0:8], AF.Exp, bias=m1, scale=1.0),
         waits=[sV.w()])
    LB.note(sS)
    P.op("vector", lambda e: e.tensor_reduce(sm, pr, axis=AX.X, op=ALU.add), waits=[sS.w()])
    P.op("vector", lambda e: e.reciprocal(sm, sm))
    P.op("vector", lambda e: e.tensor_scalar(pr, pr, sm, None, ALU.mult))
    P.op("vector", lambda e: e.tensor_copy(t1, pr))
    for _ in range(2):
        P.op("vector", lambda e: e.tensor_reduce(m1, t1, axis=AX.X, op=ALU.max))
        P.op("vector", lambda e: e.tensor_scalar(t2, t1, m1, None, ALU.is_ge))
        P.op("vector", lambda e: e.scalar_tensor_tensor(t1, t2, -1e9, t1, ALU.mult, ALU.add))
    P.op("vector", lambda e: e.tensor_reduce(m1, t1, axis=AX.X, op=ALU.max))
    P.op("vector", lambda e: e.tensor_scalar(t2, pr, m1, None, ALU.is_ge))
    P.op("vector", lambda e: e.tensor_tensor(pr, pr, t2, ALU.mult))
    P.op("vector", lambda e: e.tensor_reduce(sm, pr, axis=AX.X, op=ALU.add))
    P.op("vector", lambda e: e.reciprocal(sm, sm))
    P.op("vector", lambda e: e.tensor_scalar(pr, pr, sm, None, ALU.mult))
    w_done = sV.w()
    WB = C.pb[2]
    mm_group(P, WB, [lambda e: e.matmul(WB.h[:, 0:8], C.ones_f_row, pr, start=True, stop=True)],
             waits=[w_done])
    P.op("vector", lambda e: e.tensor_copy(C.wvec, WB.h[:, 0:8]), waits=WB.read_waits())
    WB.note(sV)
    C.wvec_done = sV.w()



def emit_l1ffn(C, x3hbm, waScr, wbScr, out_d):
    nc, P = C.nc, C.P
    sV, sS, sG = C.sV, C.sS, C.sG
    NG = 4
    GT = 1024

    acc = C.xbuf[:, 0:16384].bitcast(F32)
    xgs = [C.xbuf[:, 16384:24576], C.A2[:, 16384:24576]]
    hT = [C.A2[:, 0:4096], C.A2[:, 4096:8192]]
    C.jn_off = 8192
    outst = C.A2[:, 12288:16384].bitcast(F32)
    sW = [P.sem("l1f_w0"), P.sem("l1f_w1")]
    wfree = [[], []]
    sXg = P.sem("l1f_xg")
    sOut = P.sem("l1f_out")
    UPB = [C.pb[0], C.pb[1]]
    DNB = [C.pb[2], C.pb[3]]

    phases = [(g, z, q) for g in range(NG) for z in range(8) for q in range(4)]
    nph = len(phases)

    def emit_wload(i):
        g, z, q = phases[i]
        buf = i % 2
        dma(P, "sync", C.A1[:, 16384 * buf: 16384 * buf + 8192], waScr[z, q],
            sW[buf], waits=wfree[buf])
        dma(P, "sync", C.A1[:, 16384 * buf + 8192: 16384 * buf + 16384], wbScr[z, q],
            sW[buf])
        return sW[buf].w()

    wld = [None] * nph
    wld[0] = emit_wload(0)
    wld[1] = emit_wload(1)
    xg_war_buf = [[], []]
    for i, (g, z, q) in enumerate(phases):
        buf = i % 2
        wa_sb = C.A1[:, 16384 * buf: 16384 * buf + 8192].rearrange("p (k o) -> p k o", k=8)
        wb_sb = C.A1[:, 16384 * buf + 8192: 16384 * buf + 16384].rearrange("p (k o) -> p k o", k=8)
        xg = xgs[g % 2]
        if z == 0 and q == 0:
            xv = x3hbm[:, :].rearrange("p (k t) -> p k t", k=8)[:, :, GT * g:GT * (g + 1)]
            dma(P, "sync", xg.rearrange("p (k t) -> p k t", k=8), xv, sXg,
                waits=xg_war_buf[g % 2])
        xg_ld = sXg.w()
        for ch in range(2):
            hb = hT[ch % 2]
            for s in range(8):
                ub = UPB[s % 2]
                ems = [(lambda e, dpt=dpt, s=s, ub=ub, ch=ch, wa_sb=wa_sb, xg=xg: e.matmul(
                    ub.h[:, 0:512], wa_sb[:, dpt, 128 * s:128 * (s + 1)],
                    xg[:, 1024 * dpt + 512 * ch: 1024 * dpt + 512 * (ch + 1)],
                    start=(dpt == 0), stop=(dpt == 7))) for dpt in range(8)]
                mm_group(P, ub, ems, waits=[wld[i], xg_ld] if (ch == 0 and s < 2) else [])
                P.op("scalar", lambda e, s=s, ub=ub, hb=hb: e.activation(
                    hb[:, 512 * s:512 * (s + 1)], ub.h[:, 0:512], AF.Gelu_apprx_tanh),
                    waits=ub.read_waits())
                ub.note(sS)
            gelu_done = sS.w()
            for dpt in range(8):
                db = DNB[dpt % 2]
                ems = [(lambda e, s=s, dpt=dpt, db=db, hb=hb, wb_sb=wb_sb: e.matmul(
                    db.h[:, 0:512], wb_sb[:, s, 128 * dpt:128 * (dpt + 1)],
                    hb[:, 512 * s:512 * (s + 1)],
                    start=(s == 0), stop=(s == 7))) for s in range(8)]
                mm_group(P, db, ems, waits=[gelu_done] if dpt < 2 else [])
                if z == 0 and q == 0:
                    P.op("vector", lambda e, dpt=dpt, ch=ch, z=z, db=db: e.tensor_scalar(
                        acc[:, 1024 * dpt + 512 * ch: 1024 * dpt + 512 * (ch + 1)],
                        db.h[:, 0:512], C.wvec[:, z:z + 1], None, ALU.mult),
                        waits=db.read_waits() + ([C.wvec_done] if (ch == 0 and dpt == 0 and g == 0) else []))
                else:
                    P.op("vector", lambda e, dpt=dpt, ch=ch, z=z, db=db: e.scalar_tensor_tensor(
                        acc[:, 1024 * dpt + 512 * ch: 1024 * dpt + 512 * (ch + 1)],
                        db.h[:, 0:512], C.wvec[:, z:z + 1],
                        acc[:, 1024 * dpt + 512 * ch: 1024 * dpt + 512 * (ch + 1)],
                        ALU.mult, ALU.add),
                        waits=db.read_waits())
                db.note(sV)
        if i + 2 < nph:
            wfree[buf] = [(UPB[1].prod, UPB[1].prod.val), (DNB[1].prod, DNB[1].prod.val)]
            wld[i + 2] = emit_wload(i + 2)
        if z == 7 and q == 3:
            acc_done = sV.w()
            for jc in range(4):
                Wd = 256
                for dpt in range(8):
                    P.op("vector", lambda e, dpt=dpt, jc=jc, xg=xg: e.scalar_tensor_tensor(
                        C.F1[:, Wd * dpt:Wd * (dpt + 1)],
                        xg[:, 1024 * dpt + Wd * jc: 1024 * dpt + Wd * (jc + 1)], 2.0,
                        acc[:, 1024 * dpt + Wd * jc: 1024 * dpt + Wd * (jc + 1)],
                        ALU.mult, ALU.add),
                        waits=[acc_done] if (jc == 0 and dpt == 0) else ())
                ydone = sV.w()

                def out_writes(bcA, bcB, g=g, jc=jc):
                    for dpt in range(8):
                        P.op("vector", lambda e, dpt=dpt, bcB=bcB: e.tensor_tensor(
                            C.F1[:, 256 * dpt:256 * (dpt + 1)],
                            C.F1[:, 256 * dpt:256 * (dpt + 1)], bcB.h[:, 0:256], ALU.add))
                    x4_done = sV.w()
                    for tt in range(2):
                        for half, bk in ((0, C.pb[4]), (1, C.pb[5])):
                            wfill = bk.fill_waits() + ([x4_done] if half == 0 and tt == 0 else [])
                            for qq in range(4):
                                dpt = 4 * half + qq
                                P.op("tensor", lambda e, dpt=dpt, tt=tt, qq=qq, bk=bk: e.transpose(
                                    bk.h[:, 128 * qq:128 * (qq + 1)],
                                    C.F1[:, 256 * dpt + 128 * tt: 256 * dpt + 128 * (tt + 1)],
                                    C.eyef[:, :]),
                                    waits=wfill if qq == 0 else (),
                                    inc=bk.prod if qq == 3 else None)
                            P.op("vector", lambda e, tt=tt, half=half, bk=bk: e.tensor_copy(
                                outst[:, 1024 * tt + 512 * half: 1024 * tt + 512 * (half + 1)],
                                bk.h[:, 0:512]),
                                waits=bk.read_waits() + [sOut.w()])
                            bk.note(sV)
                        r0 = GT * g + Wd * jc + 128 * tt
                        dma(P, "sync", out_d[r0:r0 + 128, :], outst[:, 1024 * tt:1024 * (tt + 1)],
                            sOut, waits=[sV.w()])

                junction(C, [ydone], Wd, out_writes, (C.pb[0], C.pb[1], C.pb[2], C.pb[3]))
            xg_war_buf[g % 2] = [sV.w(), (UPB[0].prod, UPB[0].prod.val),
                                 (UPB[1].prod, UPB[1].prod.val)]
    C.out_dma = sOut


def finish_build(nc, C, stack):
    P = C.P
    for s in P.sems:
        s.h = stack.enter_context(nc.semaphore(s.name))
    block = stack.enter_context(nc.Block())

    def mk(name):
        def f(e):
            for fn, waits, inc, amt in P.ops[name]:
                for (s, v) in waits:
                    e.wait_ge(s.h, v)
                instr = fn(e)
                if inc is not None:
                    instr.then_inc(inc.h, amt)
        return f

    for name in P.ENGS:
        if P.ops[name]:
            getattr(block, name)(mk(name))


def build_l0attn_program():
    nc = bass.Bass("TRN2", target_bir_lowering=False)
    xin = nc.dram_tensor("xin", [T, D], F32, kind="ExternalInput").ap()
    ctx_in = nc.dram_tensor("ctx", [NCTX, DCTX], F32, kind="ExternalInput").ap()
    wq = nc.dram_tensor("wq", [D, D], F32, kind="ExternalInput").ap()
    wk = nc.dram_tensor("wk", [DCTX, D], F32, kind="ExternalInput").ap()
    wv = nc.dram_tensor("wv", [DCTX, D], F32, kind="ExternalInput").ap()
    wout = nc.dram_tensor("wout", [D, D], F32, kind="ExternalInput").ap()
    bout = nc.dram_tensor("bout", [D], F32, kind="ExternalInput").ap()
    kvK = nc.dram_tensor("kvK", [128, 8 * NCTX], BF16).ap()
    waScr = nc.dram_tensor("waScr", [4096, 8192], BF16).ap()
    wbScr = nc.dram_tensor("wbScr", [4096, 8192], BF16).ap()
    wkScr = nc.dram_tensor("wkScr", [1024, 1024], BF16).ap()
    wqScr = nc.dram_tensor("wqScr", [1024, 1024], BF16).ap()
    wvScr = nc.dram_tensor("wvScr", [1024, 1024], BF16).ap()
    woScr = nc.dram_tensor("woScr", [128, 8192], BF16).ap()
    kvV = nc.dram_tensor("kvV", [NCTX, D], BF16).ap()
    x1 = nc.dram_tensor("x1", [128, 32768], BF16, kind="ExternalOutput").ap()
    with ExitStack() as stack:
        C = Ctx()
        C.nc = nc
        C.P = Prog()
        alloc_common(nc, stack.enter_context, C)
        emit_init(C)
        emit_l0attn(C, xin, ctx_in, wq, wk, wv, wout, bout, kvK, kvV, x1)
        C.P.op("sync", lambda e: e.nop(), waits=[C.out_dma.w()], inc=None)
        finish_build(nc, C, stack)
    return nc


def build_l0ffn_program():
    nc = bass.Bass("TRN2", target_bir_lowering=False)
    x1 = nc.dram_tensor("x1", [128, 32768], BF16, kind="ExternalInput").ap()
    wfc = nc.dram_tensor("wfc", [4096, D], F32, kind="ExternalInput").ap()
    lnw = nc.dram_tensor("lnw", [4096], F32, kind="ExternalInput").ap()
    lnb = nc.dram_tensor("lnb", [4096], F32, kind="ExternalInput").ap()
    wproj = nc.dram_tensor("wproj", [D, 4096], F32, kind="ExternalInput").ap()
    x2 = nc.dram_tensor("x2", [128, 32768], BF16, kind="ExternalOutput").ap()
    with ExitStack() as stack:
        C = Ctx()
        C.nc = nc
        C.P = Prog()
        alloc_common(nc, stack.enter_context, C)
        emit_init(C)
        emit_l0ffn(C, x1, wfc, lnw, lnb, wproj, x2)
        C.P.op("sync", lambda e: e.nop(), waits=[C.out_dma.w()], inc=None)
        finish_build(nc, C, stack)
    return nc


def build_l1attn_program(replica_groups):
    nc = bass.Bass("TRN2", target_bir_lowering=False)
    x2 = nc.dram_tensor("x2", [128, 32768], BF16, kind="ExternalInput").ap()
    wq = nc.dram_tensor("wq", [D, D], F32, kind="ExternalInput").ap()
    bq = nc.dram_tensor("bq", [D], F32, kind="ExternalInput").ap()
    wk = nc.dram_tensor("wk", [D, D], F32, kind="ExternalInput").ap()
    bk = nc.dram_tensor("bk", [D], F32, kind="ExternalInput").ap()
    wv = nc.dram_tensor("wv", [D, D], F32, kind="ExternalInput").ap()
    bv = nc.dram_tensor("bv", [1, D], F32, kind="ExternalInput").ap()
    wo = nc.dram_tensor("wo", [D, D], F32, kind="ExternalInput").ap()
    bo = nc.dram_tensor("bo", [D], F32, kind="ExternalInput").ap()
    rf = nc.dram_tensor("rf", [64, 256], F32, kind="ExternalInput").ap()
    w1 = nc.dram_tensor("w1", [D, 256], F32, kind="ExternalInput").ap()
    b1 = nc.dram_tensor("b1", [256], F32, kind="ExternalInput").ap()
    w2 = nc.dram_tensor("w2", [256, 8], F32, kind="ExternalInput").ap()
    b2 = nc.dram_tensor("b2", [1, 8], F32, kind="ExternalInput").ap()
    cc_in = nc.dram_tensor("cc_in", [128, 2048], F32)
    cc_out = nc.dram_tensor("cc_out", [128, 2048], F32)
    cc2_in = nc.dram_tensor("cc2_in", [128, 8], F32)
    cc2_out = nc.dram_tensor("cc2_out", [128, 8], F32)
    x3 = nc.dram_tensor("x3", [128, 32768], BF16, kind="ExternalOutput").ap()
    wvec_o = nc.dram_tensor("wvec", [128, 8], F32, kind="ExternalOutput").ap()
    with ExitStack() as stack:
        C = Ctx()
        C.nc = nc
        C.P = Prog()
        alloc_common(nc, stack.enter_context, C)
        emit_init(C)
        emit_l1attn(C, x2, wq, bq, wk, bk, wv, bv, wo, bo, rf, x3,
                    replica_groups, cc_in, cc_out, cc2_in, cc2_out, w1, b1, w2, b2)
        sWV = C.P.sem("wvout")
        dma(C.P, "sync", wvec_o, C.wvec, sWV, waits=[C.wvec_done])
        C.P.op("sync", lambda e: e.nop(), waits=[C.out_dma.w(), sWV.w()], inc=None)
        finish_build(nc, C, stack)
    return nc


def build_l1ffn_program():
    nc = bass.Bass("TRN2", target_bir_lowering=False)
    x3 = nc.dram_tensor("x3", [128, 32768], BF16, kind="ExternalInput").ap()
    wvec_i = nc.dram_tensor("wvec_i", [128, 8], F32, kind="ExternalInput").ap()
    wa = nc.dram_tensor("wa", [8, D, 4096], F32, kind="ExternalInput").ap()
    wb = nc.dram_tensor("wb", [8, 4096, D], F32, kind="ExternalInput").ap()
    out = nc.dram_tensor("out", [T, D], F32, kind="ExternalOutput").ap()
    with ExitStack() as stack:
        C = Ctx()
        C.nc = nc
        C.P = Prog()
        alloc_common(nc, stack.enter_context, C)
        emit_init(C)
        sWv = C.P.sem("wvin")
        C.wvec = C.small[:, 48:56]
        dma(C.P, "gpsimd", C.wvec, wvec_i, sWv)
        C.wvec_done = sWv.w()
        emit_l1ffn(C, x3, wa, wb, out)
        C.P.op("sync", lambda e: e.nop(), waits=[C.out_dma.w()], inc=None)
        finish_build(nc, C, stack)
    return nc


def build_fused_program(replica_groups):
    nc = bass.Bass("TRN2", target_bir_lowering=False)
    xin = nc.dram_tensor("xin", [T, D], F32, kind="ExternalInput").ap()
    ctx_in = nc.dram_tensor("ctx", [NCTX, DCTX], F32, kind="ExternalInput").ap()
    a_wq = nc.dram_tensor("a_wq", [D, D], F32, kind="ExternalInput").ap()
    a_wk = nc.dram_tensor("a_wk", [DCTX, D], F32, kind="ExternalInput").ap()
    a_wv = nc.dram_tensor("a_wv", [DCTX, D], F32, kind="ExternalInput").ap()
    a_wout = nc.dram_tensor("a_wout", [D, D], F32, kind="ExternalInput").ap()
    a_bout = nc.dram_tensor("a_bout", [D], F32, kind="ExternalInput").ap()
    f_wfc = nc.dram_tensor("f_wfc", [4096, D], F32, kind="ExternalInput").ap()
    f_lnw = nc.dram_tensor("f_lnw", [4096], F32, kind="ExternalInput").ap()
    f_lnb = nc.dram_tensor("f_lnb", [4096], F32, kind="ExternalInput").ap()
    f_wproj = nc.dram_tensor("f_wproj", [D, 4096], F32, kind="ExternalInput").ap()
    b_wq = nc.dram_tensor("b_wq", [D, D], F32, kind="ExternalInput").ap()
    b_bq = nc.dram_tensor("b_bq", [D], F32, kind="ExternalInput").ap()
    b_wk = nc.dram_tensor("b_wk", [D, D], F32, kind="ExternalInput").ap()
    b_bk = nc.dram_tensor("b_bk", [D], F32, kind="ExternalInput").ap()
    b_wv = nc.dram_tensor("b_wv", [D, D], F32, kind="ExternalInput").ap()
    b_bv = nc.dram_tensor("b_bv", [1, D], F32, kind="ExternalInput").ap()
    b_wo = nc.dram_tensor("b_wo", [D, D], F32, kind="ExternalInput").ap()
    b_bo = nc.dram_tensor("b_bo", [D], F32, kind="ExternalInput").ap()
    b_rf = nc.dram_tensor("b_rf", [64, 256], F32, kind="ExternalInput").ap()
    r_w1 = nc.dram_tensor("r_w1", [D, 256], F32, kind="ExternalInput").ap()
    r_b1 = nc.dram_tensor("r_b1", [256], F32, kind="ExternalInput").ap()
    r_w2 = nc.dram_tensor("r_w2", [256, 8], F32, kind="ExternalInput").ap()
    r_b2 = nc.dram_tensor("r_b2", [1, 8], F32, kind="ExternalInput").ap()
    m_wa = nc.dram_tensor("m_wa", [8, D, 4096], F32, kind="ExternalInput").ap()
    m_wb = nc.dram_tensor("m_wb", [8, 4096, D], F32, kind="ExternalInput").ap()
    kvK = nc.dram_tensor("kvK", [128, 8 * NCTX], BF16).ap()
    waScr = nc.dram_tensor("waScr", [4096, 8192], BF16).ap()
    wbScr = nc.dram_tensor("wbScr", [4096, 8192], BF16).ap()
    wkScr = nc.dram_tensor("wkScr", [1024, 1024], BF16).ap()
    wqScr = nc.dram_tensor("wqScr", [1024, 1024], BF16).ap()
    wvScr = nc.dram_tensor("wvScr", [1024, 1024], BF16).ap()
    woScr = nc.dram_tensor("woScr", [128, 8192], BF16).ap()
    kvV = nc.dram_tensor("kvV", [NCTX, D], BF16).ap()
    x1h = nc.dram_tensor("x1h", [128, 32768], BF16).ap()
    x2h = nc.dram_tensor("x2h", [128, 32768], BF16).ap()
    x3h = nc.dram_tensor("x3h", [128, 32768], BF16).ap()
    cc_in = nc.dram_tensor("cc_in", [128, 2048], F32)
    cc_out = nc.dram_tensor("cc_out", [128, 2048], F32)
    cc2_in = nc.dram_tensor("cc2_in", [128, 8], F32)
    cc2_out = nc.dram_tensor("cc2_out", [128, 8], F32)
    out = nc.dram_tensor("out", [T, D], F32, kind="ExternalOutput").ap()
    with ExitStack() as stack:
        C = Ctx()
        C.nc = nc
        C.P = Prog()
        alloc_common(nc, stack.enter_context, C)
        emit_init(C)
        pieces = []
        for wd, scrt in ((b_wk, wkScr), (b_wq, wqScr), (b_wv, wvScr)):
            for p in range(8):
                pieces.append((wd[:, 128 * p:128 * (p + 1)].rearrange("(k p2) o -> p2 k o", p2=128),
                               scrt[128 * p:128 * (p + 1), :], 8))
        for k in range(8):
            pieces.append((b_wo[128 * k:128 * (k + 1), :], woScr[:, 1024 * k:1024 * (k + 1)], 1))
        for z in range(8):
            for q in range(4):
                for k in range(8):
                    pieces.append((m_wa[z][128 * k:128 * (k + 1), 1024 * q:1024 * (q + 1)],
                                   waScr[(z * 4 + q) * 128:(z * 4 + q) * 128 + 128,
                                         1024 * k:1024 * (k + 1)], 1))
                for k in range(8):
                    pieces.append((m_wb[z][1024 * q + 128 * k:1024 * q + 128 * (k + 1), :],
                                   wbScr[(z * 4 + q) * 128:(z * 4 + q) * 128 + 128,
                                         1024 * k:1024 * (k + 1)], 1))
        register_precast(C, pieces)
        emit_l0attn(C, xin, ctx_in, a_wq, a_wk, a_wv, a_wout, a_bout, kvK, kvV, x1h)
        barrier(C, extra=[C.out_dma.w()])
        emit_l0ffn(C, x1h, f_wfc, f_lnw, f_lnb, f_wproj, x2h)
        pc_done = flush_precast(C)
        barrier(C, extra=[C.out_dma.w()] + pc_done)
        scrd = {"wk": [wkScr[128 * p:128 * (p + 1), :] for p in range(8)],
                "wq": [wqScr[128 * p:128 * (p + 1), :] for p in range(8)],
                "wv": [wvScr[128 * p:128 * (p + 1), :] for p in range(8)],
                "wo": woScr}
        emit_l1attn(C, x2h, b_wq, b_bq, b_wk, b_bk, b_wv, b_bv, b_wo, b_bo, b_rf,
                    x3h, replica_groups, cc_in, cc_out, cc2_in, cc2_out,
                    r_w1, r_b1, r_w2, r_b2, scr=scrd)
        barrier(C, extra=[C.out_dma.w()])
        waS = {}
        wbS = {}
        for z in range(8):
            for q in range(4):
                waS[z, q] = waScr[(z * 4 + q) * 128:(z * 4 + q) * 128 + 128, :]
                wbS[z, q] = wbScr[(z * 4 + q) * 128:(z * 4 + q) * 128 + 128, :]
        emit_l1ffn(C, x3h, waS, wbS, out)
        C.P.op("sync", lambda e: e.nop(), waits=[C.out_dma.w()], inc=None)
        finish_build(nc, C, stack)
    return nc


_prog_cache = {}


def _get_prog(name, builder):
    if name not in _prog_cache:
        _prog_cache[name] = builder()
    return _prog_cache[name]


def kernel(**inputs):
    x = np.ascontiguousarray(inputs["L0_attn_x"].reshape(32768, 1024))
    ctx = np.ascontiguousarray(inputs["L0_attn_context"].reshape(NCTX, DCTX))
    cores = list(range(NCORES))
    nc = _get_prog("fused", lambda: build_fused_program([[0, 1], [2, 3], [4, 5], [6, 7]]))
    maps = []
    for c in range(NCORES):
        maps.append({
            "xin": np.ascontiguousarray(x[T * c:T * (c + 1)]), "ctx": ctx,
            "a_wq": inputs["L0_attn_Wq"], "a_wk": inputs["L0_attn_Wk"],
            "a_wv": inputs["L0_attn_Wv"], "a_wout": inputs["L0_attn_Wout"],
            "a_bout": inputs["L0_attn_bout"],
            "f_wfc": inputs["L0_ffn_W_fc"], "f_lnw": inputs["L0_ffn_ln_w"],
            "f_lnb": inputs["L0_ffn_ln_b"], "f_wproj": inputs["L0_ffn_W_proj"],
            "b_wq": inputs["L1_attn_Wq"], "b_bq": inputs["L1_attn_bq"],
            "b_wk": inputs["L1_attn_Wk"], "b_bk": inputs["L1_attn_bk"],
            "b_wv": inputs["L1_attn_Wv"], "b_bv": inputs["L1_attn_bv"].reshape(1, -1),
            "b_wo": inputs["L1_attn_Wo"], "b_bo": inputs["L1_attn_bo"],
            "b_rf": inputs["L1_attn_random_features"],
            "r_w1": inputs["L1_ffn_W1"], "r_b1": inputs["L1_ffn_b1"],
            "r_w2": inputs["L1_ffn_W2"], "r_b2": inputs["L1_ffn_b2"].reshape(1, -1),
            "m_wa": inputs["L1_ffn_Wa"], "m_wb": inputs["L1_ffn_Wb"]})
    res = run_bass_kernel_spmd(nc, maps, core_ids=cores)
    outs = [res.results[c]["out"] for c in range(NCORES)]
    return np.concatenate(outs, axis=0).reshape(4, 8192, 1024)


# version 12
# speedup vs baseline: 1.1069x; 1.1069x over previous
import numpy as np
import ml_dtypes
from contextlib import ExitStack

import concourse.bass as bass
import concourse.mybir as mybir
from concourse.bass_utils import run_bass_kernel_spmd

F32 = mybir.dt.float32
BF16 = mybir.dt.bfloat16
AF = mybir.ActivationFunctionType
ALU = mybir.AluOpType
AX = mybir.AxisListType

T = 4096
D = 1024
NCORES = 8
NCTX = 4096
DCTX = 768
SCALE0 = 1.5 * (128.0 ** -0.5)
EXPB = -30.0
LN_EPS = 1e-5
BF = ml_dtypes.bfloat16


class Sem:
    __slots__ = ("name", "val", "h")

    def __init__(self, name):
        self.name = name
        self.val = 0
        self.h = None

    def w(self):
        return (self, self.val)


class Prog:
    ENGS = ("tensor", "vector", "scalar", "gpsimd", "sync")

    def __init__(self):
        self.sems = []
        self.ops = {e: [] for e in self.ENGS}
        self.n = 0
        self.pending = {}
        self.sV = self.sem("sV")
        self.sS = self.sem("sS")
        self.sG = self.sem("sG")
        self.auto = {"vector": self.sV, "scalar": self.sS, "gpsimd": self.sG}

    def sem(self, name):
        s = Sem(name)
        self.sems.append(s)
        return s

    def op(self, eng, fn, waits=(), inc="auto", amt=1):
        if inc == "auto":
            inc = self.auto.get(eng)
        a = self.auto.get(eng)
        waits = list(waits) + ([a.w()] if a is not None else []) + self.pending.pop(eng, [])
        w = {}
        for (s, v) in waits:
            if v > 0:
                w[s] = max(w.get(s, 0), v)
        w = list(w.items())
        if inc is not None:
            inc.val += amt
        self.ops[eng].append((fn, w, inc, amt))
        self.n += 1


class Bank:

    def __init__(self, P, name):
        self.P = P
        self.prod = P.sem(name + "_p")
        self.reads = []
        self.h = None

    def fill_waits(self):
        w = self.reads
        self.reads = []
        return w

    def read_waits(self):
        return [(self.prod, self.prod.val)]

    def mark_read(self):
        raise NotImplementedError

    def note(self, sem):
        self.reads.append((sem, sem.val))


def mm_group(P, bank, emits, waits=()):
    w = bank.fill_waits() + list(waits)
    n = len(emits)
    for i, fn in enumerate(emits):
        P.op("tensor", fn, waits=w if i == 0 else (),
             inc=bank.prod if i == n - 1 else None, amt=1)


def dma(P, q, out_ap, in_ap, sem, waits=(), **kw):
    P.op(q, lambda e, o=out_ap, i=in_ap, kw=kw: e.dma_start(out=o, in_=i, **kw),
         waits=waits, inc=sem, amt=16)


def rearr_w(w_ap, p=128):
    return w_ap.rearrange("(k p) o -> p k o", p=p)


class Ctx:
    pass


def register_precast(C, pieces):
    P = C.P
    C.pc_list = list(pieces)
    C.pc_i = 0
    C.pc_stage = None
    C.pc_sem_c = [P.sem("pc_c0"), P.sem("pc_c1")]
    C.pc_sem_s = [P.sem("pc_s0"), P.sem("pc_s1")]
    C.pc_entry = []


def drip_precast(C, n):
    P = C.P
    if not hasattr(C, "pc_list"):
        return
    if C.pc_stage is None:
        C.pc_stage = [C.stage[:, 1024:2048], C.sm16[:, 256:1280]]
    while n > 0 and C.pc_i < len(C.pc_list):
        i = C.pc_i
        b = i % 2
        srcap, dst, kdim = C.pc_list[i]
        stg = C.pc_stage[b]
        stg_v = stg.rearrange("p (k o) -> p k o", k=kdim) if kdim > 1 else stg
        w = [C.pc_sem_s[b].w()] + (C.pc_entry if i < 2 else [])
        dma(P, "gpsimd", stg_v, srcap, C.pc_sem_c[b], waits=w)
        dma(P, "sync", dst, stg, C.pc_sem_s[b], waits=[C.pc_sem_c[b].w()])
        C.pc_i += 1
        n -= 1


def flush_precast(C):
    drip_precast(C, 1 << 30)
    if hasattr(C, "pc_list"):
        return [C.pc_sem_s[0].w(), C.pc_sem_s[1].w()]
    return []


def barrier(C, extra=()):
    P = C.P
    snap = [b.prod.w() for b in C.pb] + [b.prod.w() for b in C.tb]
    snap += [P.sV.w(), P.sS.w(), P.sG.w()] + list(extra)
    for eng in P.ENGS:
        P.pending[eng] = list(snap)


def alloc_common(nc, E, C):
    C.xbuf = E(nc.sbuf_tensor("xbuf", [128, 32768], BF16))
    C.A1 = E(nc.sbuf_tensor("A1", [128, 32768], BF16))
    C.A2 = E(nc.sbuf_tensor("A2", [128, 25600], BF16))
    C.F1 = E(nc.sbuf_tensor("F1", [128, 2048], F32))
    C.stage = E(nc.sbuf_tensor("stage", [128, 2048], BF16))
    C.fstage = E(nc.sbuf_tensor("fstage", [128, 256], F32))
    C.small = E(nc.sbuf_tensor("small", [128, 128], F32))
    C.sm16 = E(nc.sbuf_tensor("sm16", [128, 1536], BF16))
    C.eye = E(nc.sbuf_tensor("eye", [128, 128], BF16))
    C.eyef = E(nc.sbuf_tensor("eyef", [128, 128], F32))
    C.st = E(nc.sbuf_tensor("st", [1, 1536], F32))
    C.pb = [Bank(C.P, f"pb{i}") for i in range(6)]
    for i, b in enumerate(C.pb):
        b.h = E(nc.psum_tensor(f"ps{i}", [128, 512], F32))
    C.tb = [Bank(C.P, f"tb{i}") for i in range(2)]
    for i, b in enumerate(C.tb):
        b.h = E(nc.psum_tensor(f"pt{i}", [128, 1024], BF16))
    C.sV = C.P.sV
    C.sS = C.P.sS
    C.sG = C.P.sG


def emit_init(C):
    P = C.P
    io = C.fstage[:, 0:128]
    P.op("gpsimd", lambda e: e.iota(io, [[-1, 128]], base=0, channel_multiplier=1,
                                    allow_small_or_imprecise_dtypes=True))
    g0 = C.sG.w()
    P.op("vector", lambda e: e.tensor_scalar(C.eye[:, :], io, 0.0, None, ALU.is_equal),
         waits=[g0])
    eyev = C.sV.w()
    P.op("vector", lambda e: e.tensor_copy(C.eyef[:, :], C.eye[:, :]), waits=[eyev])
    P.op("vector", lambda e: e.memset(C.small[:, 0:1], EXPB))
    P.op("vector", lambda e: e.memset(C.small[:, 1:2], LN_EPS))
    P.op("vector", lambda e: e.memset(C.small[:, 2:3], 1.0))
    P.op("vector", lambda e: e.memset(C.fstage[0:1, 128:256], 1.0))
    P.op("vector", lambda e: e.memset(C.sm16[:, 0:1], 1.0))
    P.op("vector", lambda e: e.memset(C.sm16[0:1, 2:4], 1.0))
    C.expb = C.small[:, 0:1]
    C.epsv = C.small[:, 1:2]
    C.eps1 = C.small[0:1, 1:2]
    C.ones_f_col = C.small[:, 2:3]
    C.ones_f_row = C.fstage[0:1, 128:256]
    C.ones_b_col = C.sm16[:, 0:1]
    C.ones_b_1 = C.sm16[0:1, 2:3]
    C.jn_off = 8192
    C.init_done = C.sV.w()


def junction(C, y_waits, width, out_writes, banks, ybuf=None):
    P = C.P
    sV, sS = C.sV, C.sS
    stat1, stat2, bcA, bcB = banks
    if ybuf is None:
        ybuf = C.F1
    W8 = 8 * width
    ybf = C.A2[:, C.jn_off: C.jn_off + W8]
    ysq = C.A2[:, C.jn_off + W8: C.jn_off + 2 * W8]
    P.op("vector", lambda e, ybuf=ybuf: e.tensor_copy(ybf, ybuf[:, 0:W8]),
         waits=list(y_waits) + [(stat1.prod, stat1.prod.val)])
    wyb = sV.w()
    P.op("scalar", lambda e: e.activation(ysq, ybf, AF.Square),
         waits=[wyb, (stat2.prod, stat2.prod.val)])
    wys = sS.w()
    ems1 = [(lambda e, d=d: e.matmul(stat1.h[0:1, 0:width], C.ones_b_col,
                                     ybf[:, d * width:(d + 1) * width],
                                     start=(d == 0), stop=(d == 7))) for d in range(8)]
    ems2 = [(lambda e, d=d: e.matmul(stat2.h[0:1, 0:width], C.ones_b_col,
                                     ysq[:, d * width:(d + 1) * width],
                                     start=(d == 0), stop=(d == 7))) for d in range(8)]
    mm_group(P, stat1, ems1, waits=[wyb, C.init_done])
    mm_group(P, stat2, ems2, waits=[wys])
    assert width <= 256
    mu = C.st[0:1, 0:width]
    e2 = C.st[0:1, 256:256 + width]
    var = e2
    arow = C.st[0:1, 512:512 + width]
    brow = C.st[0:1, 768:768 + width]
    tmp = C.st[0:1, 1024:1024 + width]
    P.op("vector", lambda e: e.tensor_scalar(mu, stat1.h[0:1, 0:width], 1.0 / D, None, ALU.mult),
         waits=stat1.read_waits())
    stat1.note(sV)
    wmu = sV.w()
    P.op("vector", lambda e: e.tensor_scalar(e2, stat2.h[0:1, 0:width], 1.0 / D, None, ALU.mult),
         waits=stat2.read_waits())
    stat2.note(sV)
    P.op("vector", lambda e: e.tensor_tensor(tmp, mu, mu, ALU.mult), waits=[wmu])
    wtmp = sV.w()
    P.op("vector", lambda e: e.tensor_tensor(var, e2, tmp, ALU.subtract), waits=[wtmp])
    wvar = sV.w()
    P.op("scalar", lambda e: e.activation(var, var, AF.Sqrt, bias=C.eps1, scale=1.0),
         waits=[wvar])
    wsd = sS.w()
    P.op("vector", lambda e: e.reciprocal(arow, var), waits=[wsd])
    wa = sV.w()
    P.op("vector", lambda e: e.scalar_tensor_tensor(brow, mu, -1.0, arow, ALU.mult, ALU.mult),
         waits=[wa])
    wab = sV.w()
    mm_group(P, bcA, [lambda e: e.matmul(bcA.h[:, 0:width], C.ones_f_row, arow,
                                         start=True, stop=True)], waits=[wab])
    mm_group(P, bcB, [lambda e: e.matmul(bcB.h[:, 0:width], C.ones_f_row, brow,
                                         start=True, stop=True)], waits=[wab])
    rb = bcA.read_waits() + bcB.read_waits()
    for dpt in range(8):
        P.op("vector", lambda e, d=dpt, ybuf=ybuf: e.tensor_tensor(
            ybuf[:, d * width:(d + 1) * width], ybuf[:, d * width:(d + 1) * width],
            bcA.h[:, 0:width], ALU.mult),
            waits=rb if dpt == 0 else ())
    out_writes(bcA, bcB)
    bcA.note(sV)
    bcB.note(sV)


def emit_l0attn(C, xin, ctx_in, wq_d, wk_d, wv_d, wout_d, bout_d, kvK, kvV, x1hbm,
                spill_x1=True):
    nc, P = C.nc, C.P
    sV, sS, sG = C.sV, C.sS, C.sG

    stg = [C.stage[:, 0:1024], C.stage[:, 1024:2048]]
    sXin = [P.sem("l0a_xin0"), P.sem("l0a_xin1")]
    stg_free = [[], []]
    for t in range(32):
        b = t % 2
        tb = C.tb[b]
        dma(P, "gpsimd", stg[b], xin[128 * t:128 * (t + 1), :], sXin[b],
            waits=stg_free[b])
        wfill = tb.fill_waits() + [sXin[b].w(), C.init_done]
        for j in range(8):
            P.op("tensor",
                 lambda e, j=j, b=b: e.transpose(C.tb[b].h[:, 128 * j:128 * (j + 1)],
                                                 stg[b][:, 128 * j:128 * (j + 1)], C.eye[:, :]),
                 waits=wfill if j == 0 else (),
                 inc=tb.prod if j == 7 else None)
        stg_free[b] = [(tb.prod, tb.prod.val)]
        xv = C.xbuf[:, :].rearrange("p (k t) -> p k t", k=8)[:, :, 128 * t:128 * (t + 1)]
        P.op("vector", lambda e, xv=xv, tb=tb: e.tensor_copy(
            xv, tb.h[:, :].rearrange("p (k t) -> p k t", k=8)),
            waits=tb.read_waits())
        tb.note(sV)
    bout_sb = C.small[:, 16:24]
    sBout = P.sem("l0a_bout")
    dma(P, "gpsimd", bout_sb, bout_d.rearrange("(k p) -> p k", p=128), sBout,
        allow_slow_non_contiguous=True)

    wk_sb = C.A2[:, 0:6144].rearrange("p (k o) -> p k o", k=6)
    wv_sb = C.A2[:, 6144:12288].rearrange("p (k o) -> p k o", k=6)
    sWkv = P.sem("l0a_wkv")
    dma(P, "gpsimd", wk_sb, rearr_w(wk_d), sWkv)
    dma(P, "gpsimd", wv_sb, rearr_w(wv_d), sWkv)
    wkv_ld = sWkv.w()
    cst = [C.stage[:, 0:768], C.stage[:, 1024:1024 + 768]]
    sCtx = [P.sem("l0a_ctx0"), P.sem("l0a_ctx1")]
    kst = C.A2[:, 12288:13312]
    vst = C.A2[:, 13312:14336]
    ctile = C.A2[:, 14336:15104]
    kb0, kb1, vb0, vb1 = C.pb[0], C.pb[1], C.pb[2], C.pb[3]
    sKsp = P.sem("l0a_ksp")
    sVsp = P.sem("l0a_vsp")
    ctile_war = []
    for ct in range(32):
        b = ct % 2
        tb = C.tb[b]
        dma(P, "gpsimd", cst[b], ctx_in[128 * ct:128 * (ct + 1), :], sCtx[b],
            waits=stg_free[b])
        wfill = tb.fill_waits() + [sCtx[b].w()]
        for j in range(6):
            P.op("tensor",
                 lambda e, j=j, b=b: e.transpose(C.tb[b].h[:, 128 * j:128 * (j + 1)],
                                                 cst[b][:, 128 * j:128 * (j + 1)], C.eye[:, :]),
                 waits=wfill if j == 0 else (),
                 inc=tb.prod if j == 5 else None)
        stg_free[b] = [(tb.prod, tb.prod.val)]
        P.op("vector", lambda e, tb=tb: e.tensor_copy(ctile[:, :], tb.h[:, 0:768]),
             waits=tb.read_waits() + ctile_war)
        tb.note(sV)
        vC = sV.w()
        for kb, hbase in ((kb0, 0), (kb1, 4)):
            ems = []
            for hh in range(4):
                h = hbase + hh
                for dpt in range(6):
                    ems.append(lambda e, h=h, hh=hh, dpt=dpt, kb=kb: e.matmul(
                        kb.h[:, hh * 128:(hh + 1) * 128],
                        wk_sb[:, dpt, h * 128:(h + 1) * 128],
                        ctile[:, 128 * dpt:128 * (dpt + 1)],
                        start=(hh == 0 and dpt == 0), stop=(hh == 3 and dpt == 5)))
            mm_group(P, kb, ems, waits=[wkv_ld, vC])
        P.op("vector", lambda e: e.tensor_copy(kst[:, 0:512], kb0.h[:, :]),
             waits=kb0.read_waits() + [sKsp.w()])
        kb0.note(sV)
        P.op("vector", lambda e: e.tensor_copy(kst[:, 512:1024], kb1.h[:, :]),
             waits=kb1.read_waits())
        kb1.note(sV)
        kvC = sV.w()
        kout = kvK[:, :].rearrange("p (h t) -> p h t", h=8)[:, :, 128 * ct:128 * (ct + 1)]
        dma(P, "sync", kout, kst.rearrange("p (h t) -> p h t", h=8), sKsp, waits=[kvC])
        for half, vb in ((0, vb0), (1, vb1)):
            ems = [(lambda e, dpt=dpt, half=half, vb=vb: e.matmul(
                vb.h[:, :], ctile[:, 128 * dpt:128 * (dpt + 1)],
                wv_sb[:, dpt, half * 512:(half + 1) * 512],
                start=(dpt == 0), stop=(dpt == 5))) for dpt in range(6)]
            mm_group(P, vb, ems, waits=[wkv_ld, vC])
            P.op("vector", lambda e, vb=vb, half=half: e.tensor_copy(
                vst[:, half * 512:(half + 1) * 512], vb.h[:, :]),
                waits=vb.read_waits() + ([sVsp.w()] if half == 0 else []))
            vb.note(sV)
        ctile_war = [(vb1.prod, vb1.prod.val)]
        dma(P, "sync", kvV[128 * ct:128 * (ct + 1), :], vst, sVsp, waits=[sV.w()])

    kKsb = C.A2[:, 0:4096]
    kVsb = C.A2[:, 4096:8192]
    ptb = C.A2[:, 8192:24576]
    qch = C.A2[:, 24576:25600]
    sKV = P.sem("l0a_kv")
    sWq = P.sem("l0a_wq")
    QB = C.pb[0]
    SBKS = [C.pb[1], C.pb[2], C.pb[3], C.pb[4]]
    OB = C.pb[5]
    DB = C.tb[0]
    RB = C.tb[1]
    DBv = C.tb[0].h[:, :].bitcast(F32)
    RBv = C.tb[1].h[:, :].bitcast(F32)
    rcp = C.st[0:1, 1024:1536]
    kv_free = []
    wq_free = []
    q_free = [[], []]
    prev_half = []
    prev_full = []
    prev_tree = []
    spills = [sKsp.w(), sVsp.w()]
    C.pc_entry = list(stg_free[1])
    for h in range(8):
        dma(P, "sync", kKsb, kvK[:, 4096 * h:4096 * (h + 1)], sKV,
            waits=kv_free + (spills if h == 0 else []))
        dma(P, "sync", kVsb.rearrange("p (k d) -> p k d", k=32),
            kvV[:, 128 * h:128 * (h + 1)].rearrange("(k p) d -> p k d", p=128), sKV)
        ld_k = sKV.w()
        wqslice = C.stage[:, 0:1024].rearrange("p (k o) -> p k o", k=8)
        dma(P, "gpsimd", wqslice, wq_d[:, 128 * h:128 * (h + 1)].rearrange("(k p) o -> p k o", p=128),
            sWq, waits=wq_free + (stg_free[0] if h == 0 else []))
        ld_q = sWq.w()
        drip_precast(C, 30)
        for qc in range(8):
            qb = qc % 2
            ems = [(lambda e, dpt=dpt, qc=qc, ws=wqslice: e.matmul(
                QB.h[:, :], ws[:, dpt, :],
                C.xbuf[:, 4096 * dpt + 512 * qc: 4096 * dpt + 512 * (qc + 1)],
                start=(dpt == 0), stop=(dpt == 7))) for dpt in range(8)]
            mm_group(P, QB, ems, waits=[ld_q] + q_free[qb] + ([ld_k] if qc == 0 else []))
            P.op("vector", lambda e, qb=qb: e.tensor_copy(qch[:, 512 * qb:512 * (qb + 1)], QB.h[:, :]),
                 waits=QB.read_waits())
            QB.note(sV)
            qdone = sV.w()
            ob_w = OB.fill_waits()
            edone_at = {}

            def emit_O(kt):
                P.op("tensor", lambda e, kt=kt: e.matmul(
                    OB.h[:, :], kVsb[:, 128 * kt:128 * (kt + 1)], ptb[:, 512 * kt:512 * (kt + 1)],
                    start=(kt == 0), stop=(kt == 31)),
                    waits=[edone_at[kt]] + (ob_w if kt == 0 else []),
                    inc=OB.prod if (kt == 15 or kt == 31) else None)

            for kt in range(32):
                SB = SBKS[kt % 4]
                P.op("tensor", lambda e, kt=kt, SB=SB, qb=qb: e.matmul(
                    SB.h[:, :], kKsb[:, 128 * kt:128 * (kt + 1)], qch[:, 512 * qb:512 * (qb + 1)],
                    start=True, stop=True),
                    waits=SB.fill_waits() + ([qdone] if kt <= 3 else []),
                    inc=SB.prod)
                war = []
                if kt == 0:
                    war = prev_tree + prev_half
                elif kt == 16:
                    war = prev_full
                P.op("scalar", lambda e, kt=kt, SB=SB: e.activation(
                    ptb[:, 512 * kt:512 * (kt + 1)], SB.h[:, :], AF.Exp, bias=C.expb, scale=SCALE0),
                    waits=SB.read_waits() + war)
                SB.note(sS)
                e_done = sS.w()
                edone_at[kt] = e_done
                if kt >= 4:
                    emit_O(kt - 4)
                if kt == 1:
                    P.op("vector", lambda e: e.tensor_tensor(
                        C.F1[:, 0:512], ptb[:, 0:512], ptb[:, 512:1024], ALU.add),
                        waits=[e_done])
                elif kt > 1:
                    P.op("vector", lambda e, kt=kt: e.tensor_tensor(
                        C.F1[:, 0:512], C.F1[:, 0:512], ptb[:, 512 * kt:512 * (kt + 1)], ALU.add),
                        waits=[e_done])
            for kt in range(28, 32):
                emit_O(kt)
            prev_half = [(OB.prod, OB.prod.val - 1)]
            prev_full = [(OB.prod, OB.prod.val)]
            prev_tree = [sV.w()]
            treedone = sV.w()
            P.op("tensor", lambda e: e.matmul(DBv[0:1, :], C.ones_f_col, C.F1[:, 0:512],
                                              start=True, stop=True),
                 waits=DB.fill_waits() + [treedone], inc=DB.prod)
            P.op("vector", lambda e: e.reciprocal(rcp, DBv[0:1, :]), waits=DB.read_waits())
            DB.note(sV)
            rcpdone = sV.w()
            P.op("tensor", lambda e: e.matmul(RBv, C.ones_f_row, rcp, start=True, stop=True),
                 waits=RB.fill_waits() + [rcpdone], inc=RB.prod)
            P.op("vector", lambda e: e.tensor_copy(C.F1[:, 512:1024], RBv), waits=RB.read_waits())
            RB.note(sV)
            rbdone = sV.w()
            P.op("vector", lambda e, h=h, qc=qc: e.tensor_tensor(
                C.A1[:, 4096 * h + 512 * qc: 4096 * h + 512 * (qc + 1)],
                OB.h[:, :], C.F1[:, 512:1024], ALU.mult),
                waits=OB.read_waits() + [rbdone])
            OB.note(sV)
            q_free[qb] = [(OB.prod, OB.prod.val)]
        kv_free = [(OB.prod, OB.prod.val)]
        wq_free = [(QB.prod, QB.prod.val)]

    wout_sb = C.A2[:, 0:8192].rearrange("p (k o) -> p k o", k=8)
    sWo = P.sem("l0a_wo")
    dma(P, "gpsimd", wout_sb, rearr_w(wout_d), sWo, waits=kv_free)
    wout_ld = sWo.w()
    bout_w = sBout.w()
    zb_banks = [C.pb[0], C.pb[1]]
    W = 256
    sSp1 = P.sem("l0a_sp1")
    ybufs = [C.F1, C.A2[:, 12288:16384].bitcast(F32)]
    pend = None
    for ch in range(16):
        yb = ybufs[ch % 2]
        for dpt in range(8):
            zb = zb_banks[dpt % 2]
            ems = [(lambda e, ipt=ipt, dpt=dpt, ch=ch, zb=zb: e.matmul(
                zb.h[:, 0:W], wout_sb[:, ipt, 128 * dpt:128 * (dpt + 1)],
                C.A1[:, 4096 * ipt + W * ch: 4096 * ipt + W * (ch + 1)],
                start=(ipt == 0), stop=(ipt == 7))) for ipt in range(8)]
            mm_group(P, zb, ems, waits=[wout_ld] if (ch == 0 and dpt < 2) else [])
            P.op("vector", lambda e, dpt=dpt, ch=ch, zb=zb, yb=yb: e.scalar_tensor_tensor(
                yb[:, W * dpt:W * (dpt + 1)], zb.h[:, 0:W], bout_sb[:, dpt:dpt + 1],
                C.xbuf[:, 4096 * dpt + W * ch: 4096 * dpt + W * (ch + 1)], ALU.add, ALU.add),
                waits=zb.read_waits() + ([bout_w] if ch == 0 and dpt == 0 else []))
            zb.note(sV)
        ydone = sV.w()

        def out_writes(bcA, bcB, ch=ch, yb=yb):
            for dpt in range(8):
                P.op("vector", lambda e, dpt=dpt, ch=ch, bcB=bcB, yb=yb: e.tensor_tensor(
                    C.xbuf[:, 4096 * dpt + W * ch: 4096 * dpt + W * (ch + 1)],
                    yb[:, W * dpt:W * (dpt + 1)], bcB.h[:, 0:W], ALU.add),
                    waits=[sV.w()] if dpt == 0 else ())
            if spill_x1:
                xv = C.xbuf[:, :].rearrange("p (k t) -> p k t", k=8)[:, :, W * ch:W * (ch + 1)]
                ov = x1hbm[:, :].rearrange("p (k t) -> p k t", k=8)[:, :, W * ch:W * (ch + 1)]
                dma(P, "sync", ov, xv, sSp1, waits=[sV.w()])

        if pend is not None:
            junction(C, pend[0], W, pend[1], (C.pb[2], C.pb[3], C.pb[4], C.pb[5]), ybuf=pend[2])
        pend = ([ydone], out_writes, yb)
        drip_precast(C, 12)
    junction(C, pend[0], W, pend[1], (C.pb[2], C.pb[3], C.pb[4], C.pb[5]), ybuf=pend[2])
    C.out_dma = sSp1



def emit_l0ffn(C, x1hbm, wfc_d, lnw_d, lnb_d, wproj_d, x2hbm):
    nc, P = C.nc, C.P
    sV, sS, sG = C.sV, C.sS, C.sG
    NELEM = 4096.0 * 1024.0

    sWt = P.sem("l0f_wt")
    sLn = P.sem("l0f_ln")
    lnwv = C.small[:, 64:96]
    lnbv = C.sm16[:, 48:80]
    dma(P, "gpsimd", lnwv, lnw_d.rearrange("(k p) -> p k", p=128), sLn,
        allow_slow_non_contiguous=True)
    dma(P, "gpsimd", lnbv, lnb_d.rearrange("(k p) -> p k", p=128), sLn,
        allow_slow_non_contiguous=True)
    ln_ld = sLn.w()

    gacc = C.small[:, 24:56]
    g1 = C.small[:, 56:57]
    thr_fc = C.small[:, 57:58]
    thr_pj = C.small[:, 58:59]
    thr_fcn = C.small[:, 63:64]
    thr_pjn = C.small[:, 62:63]
    m1_bf = C.A2[:, 0:1024]
    m2_bf = C.A2[:, 1024:2048]
    qt = C.A2[:, 2048:3072]
    qus = C.A2[:, 3072:4096]
    wst = [C.A2[:, 4096:12288].bitcast(F32), C.A2[:, 12288:20480].bitcast(F32)]
    GB = C.pb[0]
    KB = C.pb[5]

    def gamma_pass(w4_view, ntiles, thr_dst, thr_neg):
        for b in range(ntiles // 4):
            ws = wst[b % 2]
            dma(P, "sync", ws.rearrange("p (k o) -> p k o", k=4), w4_view(b), sWt,
                waits=[sV.w()])
            ld = sWt.w()
            for j in range(4):
                P.op("vector", lambda e, b=b, j=j, ws=ws: e.tensor_reduce(
                    gacc[:, 4 * b + j:4 * b + j + 1], ws[:, 1024 * j:1024 * (j + 1)],
                    axis=AX.X, op=ALU.add, apply_absolute_value=True),
                    waits=[ld] if j == 0 else [])
        P.op("vector", lambda e, n=ntiles: e.tensor_reduce(
            g1, gacc[:, 0:n], axis=AX.X, op=ALU.add))
        gsum = sV.w()
        mm_group(P, GB, [lambda e: e.matmul(GB.h[0:1, 0:1], C.ones_f_col, g1,
                                            start=True, stop=True)],
                 waits=[gsum, C.init_done])
        P.op("vector", lambda e: e.tensor_scalar(C.st[0:1, 0:1], GB.h[0:1, 0:1],
                                                 0.5 / NELEM, 0.5e-6, ALU.mult, ALU.add),
             waits=GB.read_waits())
        GB.note(sV)
        thr1 = sV.w()
        mm_group(P, GB, [lambda e: e.matmul(GB.h[:, 1:2], C.ones_f_row, C.st[0:1, 0:1],
                                            start=True, stop=True)], waits=[thr1])
        P.op("vector", lambda e: e.tensor_copy(thr_dst, GB.h[:, 1:2]),
             waits=GB.read_waits())
        GB.note(sV)
        P.op("vector", lambda e: e.tensor_scalar(thr_neg, thr_dst, -1.0, None, ALU.mult))

    def quantize_tile(wt, thr, thr_n, prev_tp, ld):
        P.op("vector", lambda e, wt=wt, thr=thr: e.tensor_scalar(
            m1_bf, wt, thr, None, ALU.is_gt), waits=[ld])
        P.op("vector", lambda e, wt=wt, thr_n=thr_n: e.tensor_scalar(
            m2_bf, wt, thr_n, None, ALU.is_lt))
        P.op("vector", lambda e: e.tensor_tensor(qt, m1_bf, m2_bf, ALU.subtract),
             waits=prev_tp)

    gamma_pass(lambda b: wfc_d[512 * b:512 * (b + 1), :].rearrange("(k p) o -> p k o", p=128),
               32, thr_fc, thr_fcn)
    prev_tp = []
    ld4 = None
    for i in range(32):
        b, j = divmod(i, 4)
        ws = wst[b % 2]
        if j == 0:
            dma(P, "sync", ws.rearrange("p (k o) -> p k o", k=4),
                wfc_d[512 * b:512 * (b + 1), :].rearrange("(k p) o -> p k o", p=128),
                sWt, waits=[sV.w()])
            ld4 = sWt.w()
        quantize_tile(ws[:, 1024 * j:1024 * (j + 1)], thr_fc, thr_fcn, prev_tp, ld4)
        qdone = sV.w()
        tb = C.tb[i % 2]
        wfill = tb.fill_waits() + [qdone]
        for j in range(8):
            P.op("tensor", lambda e, j=j, tb=tb: e.transpose(
                tb.h[:, 128 * j:128 * (j + 1)], qt[:, 128 * j:128 * (j + 1)], C.eye[:, :]),
                waits=wfill if j == 0 else (), inc=tb.prod if j == 7 else None)
        prev_tp = [(tb.prod, tb.prod.val)]
        dst = C.A1[:, :].rearrange("p (k t) -> p k t", k=8)[:, :, 128 * i:128 * (i + 1)]
        P.op("vector", lambda e, dst=dst, tb=tb: e.tensor_copy(
            dst, tb.h[:, :].rearrange("p (k t) -> p k t", k=8)),
            waits=tb.read_waits())
        tb.note(sV)
    wfc_ready = sV.w()

    gamma_pass(lambda b: wproj_d[128 * b:128 * (b + 1), :].rearrange("p (k o) -> p k o", k=4),
               32, thr_pj, thr_pjn)
    sKmm = P.sem("l0f_kmm")
    prev_tp = []
    ld4 = None
    for i in range(32):
        j, qn = divmod(i, 4)
        ws = wst[j % 2]
        if qn == 0:
            dma(P, "sync", ws.rearrange("p (k o) -> p k o", k=4),
                wproj_d[128 * j:128 * (j + 1), :].rearrange("p (k o) -> p k o", k=4),
                sWt, waits=[sV.w()])
            ld4 = sWt.w()
        quantize_tile(ws[:, 1024 * qn:1024 * (qn + 1)], thr_pj, thr_pjn, prev_tp, ld4)
        qdone = sV.w()
        tb = C.tb[i % 2]
        wfill = tb.fill_waits() + [qdone]
        for c in range(8):
            P.op("tensor", lambda e, c=c, tb=tb: e.transpose(
                tb.h[:, 128 * c:128 * (c + 1)], qt[:, 128 * c:128 * (c + 1)], C.eye[:, :]),
                waits=wfill if c == 0 else (), inc=tb.prod if c == 7 else None)
        prev_tp = [(tb.prod, tb.prod.val)]
        P.op("vector", lambda e, tb=tb: e.tensor_copy(qus, tb.h[:, :]),
             waits=tb.read_waits() + [(sKmm, sKmm.val)])
        tb.note(sV)
        qus_done = sV.w()
        for c in range(8):
            wk = [qus_done] if c == 0 else []
            if i == 0 and c == 0:
                wk.append(ln_ld)
            if qn == 0 and c == 0:
                wk += KB.fill_waits()
            P.op("tensor", lambda e, c=c, j=j, qn=qn: e.matmul(
                KB.h[:, 0:1], qus[:, 128 * c:128 * (c + 1)],
                lnbv[:, 8 * qn + c:8 * qn + c + 1],
                start=(qn == 0 and c == 0), stop=(qn == 3 and c == 7)),
                waits=wk, inc=sKmm if c == 7 else None)
        if qn == 3:
            P.op("vector", lambda e, j=j: e.tensor_copy(C.small[:, 24 + j:25 + j], KB.h[:, 0:1]),
                 waits=[(sKmm, sKmm.val)])
            KB.note(sV)
        for c in range(8):
            P.op("vector", lambda e, c=c, j=j, qn=qn: e.tensor_scalar(
                C.xbuf[:, (8 * qn + c) * 1024 + 128 * j: (8 * qn + c) * 1024 + 128 * (j + 1)],
                qus[:, 128 * c:128 * (c + 1)], lnwv[:, 8 * qn + c:8 * qn + c + 1], None, ALU.mult),
                waits=[ln_ld] if (i == 0 and c == 0) else ())
    konst_sb = C.small[:, 24:32]
    wq_ready = [sV.w(), (sKmm, sKmm.val)]

    hbuf = C.A2[:, 0:8192]
    hT = C.A2[:, 8192:16384]
    x2st = C.A2[:, 16384:18432]
    C.jn_off = 18432
    xt = C.A2[:, 22528:24576]
    sX1 = P.sem("l0f_x1")
    sX2 = P.sem("l0f_x2")
    musum = C.small[:, 59:61]
    sqsum = C.small[:, 61:63]
    rinv = C.small[:, 36:38]
    muv = C.small[:, 38:40]
    UPB = [C.pb[0], C.pb[1]]
    DNB = [C.pb[2], C.pb[3]]
    xt_war = []
    for pr in range(16):
        for s in range(2):
            t = 2 * pr + s
            xv = x1hbm[:, :].rearrange("p (k t) -> p k t", k=8)[:, :, 128 * t:128 * (t + 1)]
            dma(P, "sync", xt[:, 1024 * s:1024 * (s + 1)].rearrange("p (k t) -> p k t", k=8),
                xv, sX1, waits=xt_war if s == 0 else [])
        x1_ld = sX1.w()
        for s in range(2):
            for hc in range(8):
                ub = UPB[hc % 2]
                ems = [(lambda e, dpt=dpt, s=s, hc=hc, ub=ub: e.matmul(
                    ub.h[:, :], xt[:, 1024 * s + 128 * dpt: 1024 * s + 128 * (dpt + 1)],
                    C.A1[:, 4096 * dpt + 512 * hc: 4096 * dpt + 512 * (hc + 1)],
                    start=(dpt == 0), stop=(dpt == 7))) for dpt in range(8)]
                mm_group(P, ub, ems, waits=[x1_ld] + ((wq_ready + [wfc_ready]) if pr == 0 and s == 0 and hc < 2 else []))
                P.op("scalar", lambda e, s=s, hc=hc, ub=ub: e.activation(
                    hbuf[:, 4096 * s + 512 * hc: 4096 * s + 512 * (hc + 1)], ub.h[:, :], AF.Gelu),
                    waits=ub.read_waits())
                ub.note(sS)
            gelu_done = sS.w()
            P.op("vector", lambda e, s=s: e.tensor_reduce(
                musum[:, s:s + 1], hbuf[:, 4096 * s:4096 * (s + 1)], axis=AX.X, op=ALU.add),
                waits=[gelu_done])
            P.op("scalar", lambda e, s=s: e.activation(
                hT[:, 4096 * s:4096 * (s + 1)], hbuf[:, 4096 * s:4096 * (s + 1)], AF.Square,
                accum_out=sqsum[:, s:s + 1]))
            P.op("vector", lambda e, s=s: e.tensor_scalar(
                muv[:, s:s + 1], musum[:, s:s + 1], 1.0 / 4096.0, None, ALU.mult))
            P.op("vector", lambda e, s=s: e.scalar_tensor_tensor(
                rinv[:, s:s + 1], muv[:, s:s + 1], -1.0, muv[:, s:s + 1], ALU.mult, ALU.mult),
                waits=[sS.w()])
            P.op("vector", lambda e, s=s: e.scalar_tensor_tensor(
                rinv[:, s:s + 1], sqsum[:, s:s + 1], 1.0 / 4096.0, rinv[:, s:s + 1],
                ALU.mult, ALU.add))
            P.op("scalar", lambda e, s=s: e.activation(
                rinv[:, s:s + 1], rinv[:, s:s + 1], AF.Sqrt, bias=C.epsv, scale=1.0),
                waits=[sV.w()])
            P.op("vector", lambda e, s=s: e.reciprocal(rinv[:, s:s + 1], rinv[:, s:s + 1]),
                 waits=[sS.w()])
            for hf in range(2):
                P.op("vector", lambda e, s=s, hf=hf: e.tensor_scalar(
                    hbuf[:, 4096 * s + 2048 * hf: 4096 * s + 2048 * (hf + 1)],
                    hbuf[:, 4096 * s + 2048 * hf: 4096 * s + 2048 * (hf + 1)],
                    muv[:, s:s + 1], rinv[:, s:s + 1], ALU.subtract, ALU.mult))
        hn_done = sV.w()
        hT_half = None
        for cb in range(16):
            tb = C.tb[cb % 2]
            wfill = tb.fill_waits() + ([hn_done] if cb < 2 else [])
            k = 0
            for cblk in (2 * cb, 2 * cb + 1):
                for s in range(2):
                    P.op("tensor", lambda e, cblk=cblk, s=s, k=k, tb=tb: e.transpose(
                        tb.h[:, 256 * (cblk % 2) + 128 * s: 256 * (cblk % 2) + 128 * (s + 1)],
                        hbuf[:, 4096 * s + 128 * cblk: 4096 * s + 128 * (cblk + 1)], C.eye[:, :]),
                        waits=wfill if k == 0 else (),
                        inc=tb.prod if k == 3 else None)
                    k += 1
            P.op("vector", lambda e, cb=cb, tb=tb: e.tensor_copy(
                hT[:, 512 * cb:512 * (cb + 1)], tb.h[:, 0:512]),
                waits=tb.read_waits())
            tb.note(sV)
            if cb == 7:
                hT_half = sV.w()
        hT_done = sV.w()
        for dpt in range(8):
            db = DNB[dpt % 2]
            ems = [(lambda e, hpt=hpt, dpt=dpt, db=db: e.matmul(
                db.h[:, 0:256], C.xbuf[:, 1024 * hpt + 128 * dpt: 1024 * hpt + 128 * (dpt + 1)],
                hT[:, 256 * hpt:256 * (hpt + 1)],
                start=(hpt == 0), stop=(hpt == 31))) for hpt in range(32)]
            for i2, fn in enumerate(ems):
                P.op("tensor", fn,
                     waits=(db.fill_waits() + [hT_half]) if i2 == 0
                     else ([hT_done] if i2 == 16 else ()),
                     inc=db.prod if i2 == 31 else None)
            P.op("vector", lambda e, dpt=dpt, db=db: e.scalar_tensor_tensor(
                C.F1[:, 256 * dpt:256 * (dpt + 1)].rearrange("p (s t) -> p s t", s=2),
                db.h[:, 0:256].rearrange("p (s t) -> p s t", s=2), konst_sb[:, dpt:dpt + 1],
                xt.rearrange("p (s k t) -> p k s t", s=2, k=8)[:, dpt], ALU.add, ALU.add),
                waits=db.read_waits())
            db.note(sV)
        ydone = sV.w()

        def out_writes(bcA, bcB, pr=pr):
            for dpt in range(8):
                P.op("vector", lambda e, dpt=dpt, bcB=bcB: e.tensor_tensor(
                    x2st[:, 256 * dpt:256 * (dpt + 1)],
                    C.F1[:, 256 * dpt:256 * (dpt + 1)], bcB.h[:, 0:256], ALU.add),
                    waits=[sX2.w()] if dpt == 0 else ())
            ov = x2hbm[:, :].rearrange("p (k t) -> p k t", k=8)[:, :, 256 * pr:256 * (pr + 1)]
            dma(P, "sync", ov, x2st.rearrange("p (k t) -> p k t", k=8), sX2, waits=[sV.w()])

        junction(C, [ydone], 256, out_writes, (C.pb[4], C.pb[5], C.pb[2], C.pb[3]))
        drip_precast(C, 6)
        xt_war = [sV.w(), (UPB[0].prod, UPB[0].prod.val), (UPB[1].prod, UPB[1].prod.val)]
    C.out_dma = sX2



def emit_l1attn(C, x2hbm, wq_d, bq_d, wk_d, bk_d, wv_d, bv_d, wo_d, bo_d, rf_d,
                x3hbm, replica_groups, cc_in, cc_out, cc2_in, cc2_out,
                w1_d, b1_d, w2_d, b2_d, load_x2=True, scr=None):
    nc, P = C.nc, C.P
    sV, sS, sG = C.sV, C.sS, C.sG

    sX2i = P.sem("l1a_x2i")
    if load_x2:
        for q in range(4):
            dma(P, "sync", C.xbuf[:, 8192 * q:8192 * (q + 1)],
                x2hbm[:, 8192 * q:8192 * (q + 1)], sX2i)
    x2_ld = sX2i.w()
    sB = P.sem("l1a_bias")
    bkv = C.small[:, 96:104]
    bk1v = C.small[:, 104:112]
    bqv = C.small[:, 112:120]
    bq1v = C.small[:, 120:128]
    bov = C.small[:, 24:32]
    dma(P, "gpsimd", bkv, bk_d.rearrange("(k p) -> p k", p=128), sB,
        allow_slow_non_contiguous=True)
    dma(P, "gpsimd", bqv, bq_d.rearrange("(k p) -> p k", p=128), sB,
        allow_slow_non_contiguous=True)
    dma(P, "gpsimd", bov, bo_d.rearrange("(k p) -> p k", p=128), sB,
        allow_slow_non_contiguous=True)
    bvrow = C.sm16[0:1, 256:1280]
    dma(P, "gpsimd", bvrow, bv_d, sB)
    rf_all = C.sm16[:, 1280:1536]
    dma(P, "gpsimd", C.sm16[0:64, 1280:1536], rf_d, sB)
    dma(P, "gpsimd", C.sm16[64:128, 1280:1536], rf_d, sB)
    ones_b_row = C.sm16[0:1, 4:132]
    ones512_b = C.stage[0:1, 0:512]
    P.op("vector", lambda e: e.memset(ones_b_row, 1.0))
    P.op("vector", lambda e: e.memset(ones512_b, 1.0))
    bias_ld = sB.w()
    P.op("vector", lambda e: e.tensor_scalar(bk1v, bkv, 1.0, None, ALU.add), waits=[bias_ld])
    P.op("vector", lambda e: e.tensor_scalar(bq1v, bqv, 1.0, None, ALU.add))
    bias_rdy = sV.w()

    Ek = C.A2[:, 0:4096]
    kvbuf = C.A2[:, 8192:12288].bitcast(F32)
    kv_bf = C.A2[:, 12288:14336]
    qf_sb = C.A2[:, 14336:15360]
    wkp = C.A2[:, 16384:17408].rearrange("p (k o) -> p k o", k=8)
    wqp = C.A2[:, 17408:18432].rearrange("p (k o) -> p k o", k=8)
    kf_sb = C.A2[:, 18432:18944]
    kf_sb2 = C.A2[:, 23040:23552]
    C.jn_off = 18944
    sWk = P.sem("l1a_wk")
    sWq = P.sem("l1a_wq")
    KPB = [C.pb[0], C.pb[1]]

    def elu_evac(bank, bcol, b1col, out_ap):
        P.op("vector", lambda e, bank=bank, bcol=bcol: e.tensor_scalar(
            C.F1[:, 0:512], bank.h[:, :], bcol, 0.0, ALU.add, ALU.min),
            waits=bank.read_waits())
        P.op("scalar", lambda e: e.activation(C.F1[:, 512:1024], C.F1[:, 0:512], AF.Exp),
             waits=[sV.w()])
        P.op("vector", lambda e, bank=bank, b1col=b1col, out_ap=out_ap: e.scalar_tensor_tensor(
            out_ap, bank.h[:, :], b1col, C.F1[:, 512:1024], ALU.add, ALU.max),
            waits=[sS.w()])
        bank.note(sV)

    sWv = P.sem("l1a_wv")
    wvfull = C.A2[:, 8192:16384]
    for p in range(8):
        dma(P, "sync", wvfull[:, 1024 * p:1024 * (p + 1)], scr["wv"][p], sWv)
    wv_ld = sWv.w()
    wvv = wvfull.rearrange("p (pr k o) -> p k pr o", pr=8, k=8)
    KPB = [C.pb[0], C.pb[1]]
    for tt in range(32):
        for half in range(2):
            vb = KPB[half]
            ems = [(lambda e, dpt=dpt, tt=tt, half=half, vb=vb: e.matmul(
                vb.h[:, :], C.xbuf[:, 4096 * dpt + 128 * tt: 4096 * dpt + 128 * (tt + 1)],
                wvv[:, dpt, 4 * half:4 * (half + 1), :],
                start=(dpt == 0), stop=False)) for dpt in range(8)]
            ems.append(lambda e, half=half, vb=vb: e.matmul(
                vb.h[:, :], ones_b_row, bvrow[0:1, 512 * half:512 * (half + 1)],
                start=False, stop=True))
            mm_group(P, vb, ems, waits=[wv_ld, x2_ld, bias_rdy] if tt == 0 else [])
            P.op("scalar", lambda e, tt=tt, half=half, vb=vb: e.copy(
                C.A1[:, 1024 * tt + 512 * half: 1024 * tt + 512 * (half + 1)], vb.h[:, :]),
                waits=vb.read_waits())
            vb.note(sS)
    vfull_done = sS.w()

    wk_free = []
    for p in range(8):
        dma(P, "sync", C.A2[:, 16384:17408], scr["wk"][p], sWk, waits=wk_free)
        wk_ld = sWk.w()
        for qc in range(8):
            kb = KPB[qc % 2]
            ems = [(lambda e, dpt=dpt, qc=qc, kb=kb: e.matmul(
                kb.h[:, :], wkp[:, dpt, :],
                C.xbuf[:, 4096 * dpt + 512 * qc: 4096 * dpt + 512 * (qc + 1)],
                start=(dpt == 0), stop=(dpt == 7))) for dpt in range(8)]
            mm_group(P, kb, ems, waits=[wk_ld] + ([x2_ld, bias_rdy] if p == 0 and qc < 2 else []))
            elu_evac(kb, bkv[:, p:p + 1], bk1v[:, p:p + 1], Ek[:, 512 * qc:512 * (qc + 1)])
        ek_done = sV.w()
        KFB = [C.pb[0], C.pb[1]]
        kfs = [kf_sb, kf_sb2]
        for h2 in range(2):
            banks = (C.pb[2], C.pb[3]) if h2 == 0 else (C.pb[4], C.pb[5])
            kvw = banks[0].fill_waits() + banks[1].fill_waits()
            for kt2 in range(16):
                fb = KFB[kt2 % 2]
                kf = kfs[kt2 % 2]
                for u in range(2):
                    kt = 2 * kt2 + u
                    P.op("tensor", lambda e, h2=h2, kt=kt, u=u, fb=fb: e.matmul(
                        fb.h[:, 256 * u:256 * (u + 1)],
                        Ek[64 * h2:64 * (h2 + 1), 128 * kt:128 * (kt + 1)],
                        rf_all[64 * h2:64 * (h2 + 1), :], start=(u == 0), stop=(u == 1)),
                        waits=(fb.fill_waits() + ([ek_done, vfull_done] + kvw
                                                  if (h2 == 0 and kt2 <= 1) else []))
                        if u == 0 else [],
                        inc=fb.prod if u == 1 else None)
                P.op("scalar", lambda e, fb=fb, kf=kf: e.copy(kf[:, 0:512], fb.h[:, 0:512]),
                     waits=fb.read_waits())
                fb.note(sS)
                kf_done = sS.w()
                for u in range(2):
                    kt = 2 * kt2 + u
                    for fh in range(2):
                        P.op("tensor", lambda e, h2=h2, kt=kt, u=u, fh=fh, p=p,
                             kvb=banks[fh], kf=kf: e.matmul(
                            kvb.h[:, 0:64], kf[:, 256 * u + 128 * fh: 256 * u + 128 * (fh + 1)],
                            C.A1[:, 1024 * kt + 128 * p + 64 * h2: 1024 * kt + 128 * p + 64 * (h2 + 1)],
                            start=(kt == 0), stop=(kt == 31)),
                            waits=[kf_done] if (u == 0 and fh == 0) else [],
                            inc=banks[fh].prod if (kt == 31 and u == 1) else None)
            h = 2 * p + h2
            for fh in range(2):
                P.op("vector", lambda e, h=h, fh=fh, kvb=banks[fh]: e.tensor_copy(
                    kvbuf[:, 1024 * fh + 64 * h: 1024 * fh + 64 * (h + 1)],
                    kvb.h[:, 0:64]),
                    waits=banks[fh].read_waits())
                banks[fh].note(sV)
        wk_free = [(KPB[1].prod, KPB[1].prod.val), (KFB[1].prod, KFB[1].prod.val)]
    kv_done = sV.w()

    sKout = P.sem("l1a_kvout")
    sKin = P.sem("l1a_kvin")
    sCC = P.sem("l1a_cc")
    dma(P, "gpsimd", cc_in[:, :], kvbuf, sKout, waits=[kv_done])
    P.op("gpsimd", lambda e: e.collective_compute(
        "AllReduce", ALU.add, replica_groups=replica_groups,
        ins=[cc_in.ap().opt()], outs=[cc_out.ap().opt()]),
        waits=[sKout.w()], inc=sCC, amt=1)
    cc_trig = sCC.w()

    sWoB = P.sem("l1a_wob")
    wo_sb = C.A2[:, 0:8192].rearrange("p (k o) -> p k o", k=8)
    ek_vp_done = [(C.pb[5].prod, C.pb[5].prod.val), sV.w(), sKout.w()]
    if scr is not None:
        dma(P, "sync", wo_sb, scr["wo"].rearrange("p (k o) -> p k o", k=8), sWoB,
            waits=ek_vp_done)
    wq_free = []
    for p in range(8):
        if scr is not None:
            dma(P, "sync", C.A2[:, 17408:18432], scr["wq"][p], sWq, waits=wq_free)
        else:
            dma(P, "gpsimd", wqp, wq_d[:, 128 * p:128 * (p + 1)].rearrange("(k p) o -> p k o", p=128),
                sWq, waits=wq_free)
        wq_ld = sWq.w()
        for qc in range(8):
            qb = KPB[qc % 2]
            ems = [(lambda e, dpt=dpt, qc=qc, qb=qb: e.matmul(
                qb.h[:, :], wqp[:, dpt, :],
                C.xbuf[:, 4096 * dpt + 512 * qc: 4096 * dpt + 512 * (qc + 1)],
                start=(dpt == 0), stop=(dpt == 7))) for dpt in range(8)]
            mm_group(P, qb, ems, waits=[wq_ld])
            elu_evac(qb, bqv[:, p:p + 1], bq1v[:, p:p + 1],
                     C.A1[:, 4096 * p + 512 * qc: 4096 * p + 512 * (qc + 1)])
        wq_free = [(KPB[1].prod, KPB[1].prod.val)]
    eq_done = sV.w()
    if scr is None:
        dma(P, "gpsimd", wo_sb, rearr_w(wo_d), sWoB, waits=ek_vp_done)

    dma(P, "gpsimd", kvbuf, cc_out[:, :], sKin, waits=[cc_trig])
    P.op("vector", lambda e: e.tensor_copy(kv_bf, kvbuf), waits=[sKin.w()])
    kvbf_done = sV.w()

    QF = [C.pb[0], C.pb[1]]
    OB2 = [C.pb[2], C.pb[3]]
    for p in range(8):
        for h2 in range(2):
            for qc in range(8):
                for fh in range(2):
                    qfb = QF[fh]
                    mm_group(P, qfb, [lambda e, p=p, h2=h2, qc=qc, fh=fh, qfb=qfb: e.matmul(
                        qfb.h[:, :], rf_all[64 * h2:64 * (h2 + 1), 128 * fh:128 * (fh + 1)],
                        C.A1[64 * h2:64 * (h2 + 1), 4096 * p + 512 * qc: 4096 * p + 512 * (qc + 1)],
                        start=True, stop=True)],
                        waits=[eq_done] if (p == 0 and h2 == 0 and qc == 0) else [])
                    P.op("vector", lambda e, fh=fh, qfb=qfb: e.tensor_copy(
                        qf_sb[:, 512 * fh:512 * (fh + 1)], qfb.h[:, :]),
                        waits=qfb.read_waits())
                    qfb.note(sV)
                qf_done = sV.w()
                ob = OB2[qc % 2]
                h = 2 * p + h2
                ems = [(lambda e, h=h, fh=fh2, ob=ob: e.matmul(
                    ob.h[0:64, :], kv_bf[:, 1024 * fh + 64 * h: 1024 * fh + 64 * (h + 1)],
                    qf_sb[:, 512 * fh:512 * (fh + 1)],
                    start=(fh == 0), stop=(fh == 1))) for fh2 in range(2)]
                mm_group(P, ob, ems,
                         waits=[qf_done] + ([kvbf_done] if p == 0 and h2 == 0 and qc == 0 else []))
                P.op("vector", lambda e, h2=h2, p=p, qc=qc, ob=ob: e.tensor_copy(
                    C.A1[64 * h2:64 * (h2 + 1), 4096 * p + 512 * qc: 4096 * p + 512 * (qc + 1)],
                    ob.h[0:64, :]),
                    waits=ob.read_waits())
                ob.note(sV)
    out_done = sV.w()

    wo_ld = sWoB.w()
    zb_banks = [C.pb[0], C.pb[1]]
    W = 256
    sSp3 = P.sem("l1a_sp3")
    ybufs = [C.F1, C.A2[:, 8192:12288].bitcast(F32)]
    pend = None
    for ch in range(16):
        yb = ybufs[ch % 2]
        for dpt in range(8):
            zb = zb_banks[dpt % 2]
            ems = [(lambda e, ipt=ipt, dpt=dpt, ch=ch, zb=zb: e.matmul(
                zb.h[:, 0:W], wo_sb[:, ipt, 128 * dpt:128 * (dpt + 1)],
                C.A1[:, 4096 * ipt + W * ch: 4096 * ipt + W * (ch + 1)],
                start=(ipt == 0), stop=(ipt == 7))) for ipt in range(8)]
            mm_group(P, zb, ems, waits=[wo_ld, out_done] if (ch == 0 and dpt < 2) else [])
            P.op("vector", lambda e, dpt=dpt, ch=ch, zb=zb, yb=yb: e.scalar_tensor_tensor(
                yb[:, W * dpt:W * (dpt + 1)], zb.h[:, 0:W], bov[:, dpt:dpt + 1],
                C.xbuf[:, 4096 * dpt + W * ch: 4096 * dpt + W * (ch + 1)], ALU.add, ALU.add),
                waits=zb.read_waits())
            zb.note(sV)
        ydone = sV.w()

        def out_writes(bcA, bcB, ch=ch, yb=yb):
            for dpt in range(8):
                P.op("vector", lambda e, dpt=dpt, ch=ch, bcB=bcB, yb=yb: e.tensor_tensor(
                    C.xbuf[:, 4096 * dpt + W * ch: 4096 * dpt + W * (ch + 1)],
                    yb[:, W * dpt:W * (dpt + 1)], bcB.h[:, 0:W], ALU.add))
            xv = C.xbuf[:, :].rearrange("p (k t) -> p k t", k=8)[:, :, W * ch:W * (ch + 1)]
            ov = x3hbm[:, :].rearrange("p (k t) -> p k t", k=8)[:, :, W * ch:W * (ch + 1)]
            dma(P, "sync", ov, xv, sSp3, waits=[sV.w()])

        if pend is not None:
            junction(C, pend[0], W, pend[1], (C.pb[2], C.pb[3], C.pb[4], C.pb[5]), ybuf=pend[2])
        pend = ([ydone], out_writes, yb)
    junction(C, pend[0], W, pend[1], (C.pb[2], C.pb[3], C.pb[4], C.pb[5]), ybuf=pend[2])
    x3_done = sV.w()

    pooled = C.small[:, 32:40]
    for dpt in range(8):
        P.op("vector", lambda e, dpt=dpt: e.tensor_reduce(
            pooled[:, dpt:dpt + 1], C.xbuf[:, 4096 * dpt:4096 * (dpt + 1)],
            axis=AX.X, op=ALU.add), waits=[x3_done] if dpt == 0 else ())
    P.op("vector", lambda e: e.tensor_scalar(pooled, pooled, 1.0 / 8192.0, None, ALU.mult))
    pooled_done = sV.w()
    sPout = P.sem("l1a_pout")
    sPin = P.sem("l1a_pin")
    dma(P, "gpsimd", cc2_in[:, :], pooled, sPout, waits=[pooled_done])
    P.op("gpsimd", lambda e: e.collective_compute(
        "AllReduce", ALU.add, replica_groups=replica_groups,
        ins=[cc2_in.ap().opt()], outs=[cc2_out.ap().opt()]),
        waits=[sPout.w()], inc=sCC, amt=1)
    dma(P, "gpsimd", pooled, cc2_out[:, :], sPin, waits=[sCC.w()])
    p_ld = sPin.w()
    C.wvec = C.small[:, 48:56]
    emit_router(C, p_ld, pooled, w1_d, b1_d, w2_d, b2_d)
    C.out_dma = sSp3


def emit_router(C, p_ld, pooled, w1_d, b1_d, w2_d, b2_d):
    P = C.P
    sV, sS = C.sV, C.sS
    sRW = P.sem("rt_w")
    w1_sb = C.stage[:, 0:2048].rearrange("p (k o) -> p k o", k=8)
    w2_sb = C.sm16[:, 140:156].rearrange("p (k o) -> p k o", k=2)
    b1v = C.small[:, 40:42]
    b2bf = C.sm16[0:1, 160:168]
    pooled_bf = C.sm16[:, 132:140]
    h1bf = C.sm16[:, 156:158]
    dma(P, "gpsimd", w1_sb, rearr_w(w1_d), sRW)
    dma(P, "gpsimd", w2_sb, rearr_w(w2_d), sRW)
    dma(P, "gpsimd", b1v, b1_d.rearrange("(k p) -> p k", p=128), sRW,
        allow_slow_non_contiguous=True)
    dma(P, "gpsimd", b2bf, b2_d, sRW)
    rw_ld = sRW.w()
    P.op("vector", lambda e: e.tensor_copy(pooled_bf, pooled), waits=[p_ld])
    pb_done = sV.w()
    HB = C.pb[0]
    hw = HB.fill_waits()
    for hs in range(2):
        for dpt in range(8):
            P.op("tensor", lambda e, dpt=dpt, hs=hs: e.matmul(
                HB.h[:, hs:hs + 1], w1_sb[:, dpt, 128 * hs:128 * (hs + 1)],
                pooled_bf[:, dpt:dpt + 1],
                start=(hs == 0 and dpt == 0), stop=(hs == 1 and dpt == 7)),
                waits=(hw + [rw_ld, pb_done]) if hs == 0 and dpt == 0 else [],
                inc=HB.prod if (hs == 1 and dpt == 7) else None)
    for hs in range(2):
        P.op("scalar", lambda e, hs=hs: e.activation(
            h1bf[:, hs:hs + 1], HB.h[:, hs:hs + 1], AF.Tanh, bias=b1v[:, hs:hs + 1], scale=1.0),
            waits=HB.read_waits() if hs == 0 else ())
    HB.note(sS)
    h1_done = sS.w()
    LB = C.pb[1]
    ems = [(lambda e, hs=hs: e.matmul(
        LB.h[0:1, 0:8], h1bf[:, hs:hs + 1], w2_sb[:, hs, :],
        start=(hs == 0), stop=False)) for hs in range(2)]
    ems.append(lambda e: e.matmul(LB.h[0:1, 0:8], C.ones_b_1, b2bf,
                                  start=False, stop=True))
    mm_group(P, LB, ems, waits=[h1_done])
    pr = C.st[0:1, 1288:1296]
    t1 = C.st[0:1, 1296:1304]
    t2 = C.st[0:1, 1304:1312]
    m1 = C.st[0:1, 1312:1313]
    sm = C.st[0:1, 1313:1314]
    P.op("vector", lambda e: e.tensor_reduce(m1, LB.h[0:1, 0:8], axis=AX.X, op=ALU.max,
                                             negate=True), waits=LB.read_waits())
    LB.note(sV)
    P.op("scalar", lambda e: e.activation(pr, LB.h[0:1, 0:8], AF.Exp, bias=m1, scale=1.0),
         waits=[sV.w()])
    LB.note(sS)
    P.op("vector", lambda e: e.tensor_reduce(sm, pr, axis=AX.X, op=ALU.add), waits=[sS.w()])
    P.op("vector", lambda e: e.reciprocal(sm, sm))
    P.op("vector", lambda e: e.tensor_scalar(pr, pr, sm, None, ALU.mult))
    P.op("vector", lambda e: e.tensor_copy(t1, pr))
    for _ in range(2):
        P.op("vector", lambda e: e.tensor_reduce(m1, t1, axis=AX.X, op=ALU.max))
        P.op("vector", lambda e: e.tensor_scalar(t2, t1, m1, None, ALU.is_ge))
        P.op("vector", lambda e: e.scalar_tensor_tensor(t1, t2, -1e9, t1, ALU.mult, ALU.add))
    P.op("vector", lambda e: e.tensor_reduce(m1, t1, axis=AX.X, op=ALU.max))
    P.op("vector", lambda e: e.tensor_scalar(t2, pr, m1, None, ALU.is_ge))
    P.op("vector", lambda e: e.tensor_tensor(pr, pr, t2, ALU.mult))
    P.op("vector", lambda e: e.tensor_reduce(sm, pr, axis=AX.X, op=ALU.add))
    P.op("vector", lambda e: e.reciprocal(sm, sm))
    P.op("vector", lambda e: e.tensor_scalar(pr, pr, sm, None, ALU.mult))
    w_done = sV.w()
    WB = C.pb[2]
    mm_group(P, WB, [lambda e: e.matmul(WB.h[:, 0:8], C.ones_f_row, pr, start=True, stop=True)],
             waits=[w_done])
    P.op("vector", lambda e: e.tensor_copy(C.wvec, WB.h[:, 0:8]), waits=WB.read_waits())
    WB.note(sV)
    C.wvec_done = sV.w()



def emit_l1ffn(C, x3hbm, waScr, wbScr, out_d):
    nc, P = C.nc, C.P
    sV, sS, sG = C.sV, C.sS, C.sG
    NG = 4
    GT = 1024

    acc = C.xbuf[:, 0:16384].bitcast(F32)
    xgs = [C.xbuf[:, 16384:24576], C.A2[:, 16384:24576]]
    hT = [C.A2[:, 0:4096], C.A2[:, 4096:8192]]
    C.jn_off = 8192
    outst = C.A2[:, 12288:16384].bitcast(F32)
    sW = [P.sem("l1f_w0"), P.sem("l1f_w1")]
    wfree = [[], []]
    sXg = P.sem("l1f_xg")
    sOut = P.sem("l1f_out")
    UPB = [C.pb[0], C.pb[1]]
    DNB = [C.pb[2], C.pb[3]]

    phases = [(g, z, q) for g in range(NG) for z in range(8) for q in range(4)]
    nph = len(phases)

    def emit_wload(i):
        g, z, q = phases[i]
        buf = i % 2
        dma(P, "sync", C.A1[:, 16384 * buf: 16384 * buf + 8192], waScr[z, q],
            sW[buf], waits=wfree[buf])
        dma(P, "sync", C.A1[:, 16384 * buf + 8192: 16384 * buf + 16384], wbScr[z, q],
            sW[buf])
        return sW[buf].w()

    wld = [None] * nph
    wld[0] = emit_wload(0)
    wld[1] = emit_wload(1)
    xg_war_buf = [[], []]
    for i, (g, z, q) in enumerate(phases):
        buf = i % 2
        wa_sb = C.A1[:, 16384 * buf: 16384 * buf + 8192].rearrange("p (k o) -> p k o", k=8)
        wb_sb = C.A1[:, 16384 * buf + 8192: 16384 * buf + 16384].rearrange("p (k o) -> p k o", k=8)
        xg = xgs[g % 2]
        if z == 0 and q == 0:
            xv = x3hbm[:, :].rearrange("p (k t) -> p k t", k=8)[:, :, GT * g:GT * (g + 1)]
            dma(P, "sync", xg.rearrange("p (k t) -> p k t", k=8), xv, sXg,
                waits=xg_war_buf[g % 2])
        xg_ld = sXg.w()
        for ch in range(2):
            hb = hT[ch % 2]
            for s in range(8):
                ub = UPB[s % 2]
                ems = [(lambda e, dpt=dpt, s=s, ub=ub, ch=ch, wa_sb=wa_sb, xg=xg: e.matmul(
                    ub.h[:, 0:512], wa_sb[:, dpt, 128 * s:128 * (s + 1)],
                    xg[:, 1024 * dpt + 512 * ch: 1024 * dpt + 512 * (ch + 1)],
                    start=(dpt == 0), stop=(dpt == 7))) for dpt in range(8)]
                mm_group(P, ub, ems, waits=[wld[i], xg_ld] if (ch == 0 and s < 2) else [])
                P.op("scalar", lambda e, s=s, ub=ub, hb=hb: e.activation(
                    hb[:, 512 * s:512 * (s + 1)], ub.h[:, 0:512], AF.Gelu_apprx_tanh),
                    waits=ub.read_waits())
                ub.note(sS)
            gelu_done = sS.w()
            for dpt in range(8):
                db = DNB[dpt % 2]
                ems = [(lambda e, s=s, dpt=dpt, db=db, hb=hb, wb_sb=wb_sb: e.matmul(
                    db.h[:, 0:512], wb_sb[:, s, 128 * dpt:128 * (dpt + 1)],
                    hb[:, 512 * s:512 * (s + 1)],
                    start=(s == 0), stop=(s == 7))) for s in range(8)]
                mm_group(P, db, ems, waits=[gelu_done] if dpt < 2 else [])
                if z == 0 and q == 0:
                    P.op("vector", lambda e, dpt=dpt, ch=ch, z=z, db=db: e.tensor_scalar(
                        acc[:, 1024 * dpt + 512 * ch: 1024 * dpt + 512 * (ch + 1)],
                        db.h[:, 0:512], C.wvec[:, z:z + 1], None, ALU.mult),
                        waits=db.read_waits() + ([C.wvec_done] if (ch == 0 and dpt == 0 and g == 0) else []))
                else:
                    P.op("vector", lambda e, dpt=dpt, ch=ch, z=z, db=db: e.scalar_tensor_tensor(
                        acc[:, 1024 * dpt + 512 * ch: 1024 * dpt + 512 * (ch + 1)],
                        db.h[:, 0:512], C.wvec[:, z:z + 1],
                        acc[:, 1024 * dpt + 512 * ch: 1024 * dpt + 512 * (ch + 1)],
                        ALU.mult, ALU.add),
                        waits=db.read_waits())
                db.note(sV)
        if i + 2 < nph:
            wfree[buf] = [(UPB[1].prod, UPB[1].prod.val), (DNB[1].prod, DNB[1].prod.val)]
            wld[i + 2] = emit_wload(i + 2)
        if z == 7 and q == 3:
            acc_done = sV.w()
            for jc in range(4):
                Wd = 256
                for dpt in range(8):
                    P.op("vector", lambda e, dpt=dpt, jc=jc, xg=xg: e.scalar_tensor_tensor(
                        C.F1[:, Wd * dpt:Wd * (dpt + 1)],
                        xg[:, 1024 * dpt + Wd * jc: 1024 * dpt + Wd * (jc + 1)], 2.0,
                        acc[:, 1024 * dpt + Wd * jc: 1024 * dpt + Wd * (jc + 1)],
                        ALU.mult, ALU.add),
                        waits=[acc_done] if (jc == 0 and dpt == 0) else ())
                ydone = sV.w()

                def out_writes(bcA, bcB, g=g, jc=jc):
                    for dpt in range(8):
                        P.op("vector", lambda e, dpt=dpt, bcB=bcB: e.tensor_tensor(
                            C.F1[:, 256 * dpt:256 * (dpt + 1)],
                            C.F1[:, 256 * dpt:256 * (dpt + 1)], bcB.h[:, 0:256], ALU.add))
                    x4_done = sV.w()
                    for tt in range(2):
                        for half, bk in ((0, C.pb[4]), (1, C.pb[5])):
                            wfill = bk.fill_waits() + ([x4_done] if half == 0 and tt == 0 else [])
                            for qq in range(4):
                                dpt = 4 * half + qq
                                P.op("tensor", lambda e, dpt=dpt, tt=tt, qq=qq, bk=bk: e.transpose(
                                    bk.h[:, 128 * qq:128 * (qq + 1)],
                                    C.F1[:, 256 * dpt + 128 * tt: 256 * dpt + 128 * (tt + 1)],
                                    C.eyef[:, :]),
                                    waits=wfill if qq == 0 else (),
                                    inc=bk.prod if qq == 3 else None)
                            P.op("vector", lambda e, tt=tt, half=half, bk=bk: e.tensor_copy(
                                outst[:, 1024 * tt + 512 * half: 1024 * tt + 512 * (half + 1)],
                                bk.h[:, 0:512]),
                                waits=bk.read_waits() + [sOut.w()])
                            bk.note(sV)
                        r0 = GT * g + Wd * jc + 128 * tt
                        dma(P, "sync", out_d[r0:r0 + 128, :], outst[:, 1024 * tt:1024 * (tt + 1)],
                            sOut, waits=[sV.w()])

                junction(C, [ydone], Wd, out_writes, (C.pb[0], C.pb[1], C.pb[2], C.pb[3]))
            xg_war_buf[g % 2] = [sV.w(), (UPB[0].prod, UPB[0].prod.val),
                                 (UPB[1].prod, UPB[1].prod.val)]
    C.out_dma = sOut


def finish_build(nc, C, stack):
    P = C.P
    for s in P.sems:
        s.h = stack.enter_context(nc.semaphore(s.name))
    block = stack.enter_context(nc.Block())

    def mk(name):
        def f(e):
            for fn, waits, inc, amt in P.ops[name]:
                for (s, v) in waits:
                    e.wait_ge(s.h, v)
                instr = fn(e)
                if inc is not None:
                    instr.then_inc(inc.h, amt)
        return f

    for name in P.ENGS:
        if P.ops[name]:
            getattr(block, name)(mk(name))


def build_l0attn_program():
    nc = bass.Bass("TRN2", target_bir_lowering=False)
    xin = nc.dram_tensor("xin", [T, D], F32, kind="ExternalInput").ap()
    ctx_in = nc.dram_tensor("ctx", [NCTX, DCTX], F32, kind="ExternalInput").ap()
    wq = nc.dram_tensor("wq", [D, D], F32, kind="ExternalInput").ap()
    wk = nc.dram_tensor("wk", [DCTX, D], F32, kind="ExternalInput").ap()
    wv = nc.dram_tensor("wv", [DCTX, D], F32, kind="ExternalInput").ap()
    wout = nc.dram_tensor("wout", [D, D], F32, kind="ExternalInput").ap()
    bout = nc.dram_tensor("bout", [D], F32, kind="ExternalInput").ap()
    kvK = nc.dram_tensor("kvK", [128, 8 * NCTX], BF16).ap()
    waScr = nc.dram_tensor("waScr", [4096, 8192], BF16).ap()
    wbScr = nc.dram_tensor("wbScr", [4096, 8192], BF16).ap()
    wkScr = nc.dram_tensor("wkScr", [1024, 1024], BF16).ap()
    wqScr = nc.dram_tensor("wqScr", [1024, 1024], BF16).ap()
    wvScr = nc.dram_tensor("wvScr", [1024, 1024], BF16).ap()
    woScr = nc.dram_tensor("woScr", [128, 8192], BF16).ap()
    kvV = nc.dram_tensor("kvV", [NCTX, D], BF16).ap()
    x1 = nc.dram_tensor("x1", [128, 32768], BF16, kind="ExternalOutput").ap()
    with ExitStack() as stack:
        C = Ctx()
        C.nc = nc
        C.P = Prog()
        alloc_common(nc, stack.enter_context, C)
        emit_init(C)
        emit_l0attn(C, xin, ctx_in, wq, wk, wv, wout, bout, kvK, kvV, x1)
        C.P.op("sync", lambda e: e.nop(), waits=[C.out_dma.w()], inc=None)
        finish_build(nc, C, stack)
    return nc


def build_l0ffn_program():
    nc = bass.Bass("TRN2", target_bir_lowering=False)
    x1 = nc.dram_tensor("x1", [128, 32768], BF16, kind="ExternalInput").ap()
    wfc = nc.dram_tensor("wfc", [4096, D], F32, kind="ExternalInput").ap()
    lnw = nc.dram_tensor("lnw", [4096], F32, kind="ExternalInput").ap()
    lnb = nc.dram_tensor("lnb", [4096], F32, kind="ExternalInput").ap()
    wproj = nc.dram_tensor("wproj", [D, 4096], F32, kind="ExternalInput").ap()
    x2 = nc.dram_tensor("x2", [128, 32768], BF16, kind="ExternalOutput").ap()
    with ExitStack() as stack:
        C = Ctx()
        C.nc = nc
        C.P = Prog()
        alloc_common(nc, stack.enter_context, C)
        emit_init(C)
        emit_l0ffn(C, x1, wfc, lnw, lnb, wproj, x2)
        C.P.op("sync", lambda e: e.nop(), waits=[C.out_dma.w()], inc=None)
        finish_build(nc, C, stack)
    return nc


def build_l1attn_program(replica_groups):
    nc = bass.Bass("TRN2", target_bir_lowering=False)
    x2 = nc.dram_tensor("x2", [128, 32768], BF16, kind="ExternalInput").ap()
    wq = nc.dram_tensor("wq", [D, D], F32, kind="ExternalInput").ap()
    bq = nc.dram_tensor("bq", [D], F32, kind="ExternalInput").ap()
    wk = nc.dram_tensor("wk", [D, D], F32, kind="ExternalInput").ap()
    bk = nc.dram_tensor("bk", [D], F32, kind="ExternalInput").ap()
    wv = nc.dram_tensor("wv", [D, D], F32, kind="ExternalInput").ap()
    bv = nc.dram_tensor("bv", [1, D], F32, kind="ExternalInput").ap()
    wo = nc.dram_tensor("wo", [D, D], F32, kind="ExternalInput").ap()
    bo = nc.dram_tensor("bo", [D], F32, kind="ExternalInput").ap()
    rf = nc.dram_tensor("rf", [64, 256], F32, kind="ExternalInput").ap()
    w1 = nc.dram_tensor("w1", [D, 256], F32, kind="ExternalInput").ap()
    b1 = nc.dram_tensor("b1", [256], F32, kind="ExternalInput").ap()
    w2 = nc.dram_tensor("w2", [256, 8], F32, kind="ExternalInput").ap()
    b2 = nc.dram_tensor("b2", [1, 8], F32, kind="ExternalInput").ap()
    cc_in = nc.dram_tensor("cc_in", [128, 2048], F32)
    cc_out = nc.dram_tensor("cc_out", [128, 2048], F32)
    cc2_in = nc.dram_tensor("cc2_in", [128, 8], F32)
    cc2_out = nc.dram_tensor("cc2_out", [128, 8], F32)
    x3 = nc.dram_tensor("x3", [128, 32768], BF16, kind="ExternalOutput").ap()
    wvec_o = nc.dram_tensor("wvec", [128, 8], F32, kind="ExternalOutput").ap()
    with ExitStack() as stack:
        C = Ctx()
        C.nc = nc
        C.P = Prog()
        alloc_common(nc, stack.enter_context, C)
        emit_init(C)
        emit_l1attn(C, x2, wq, bq, wk, bk, wv, bv, wo, bo, rf, x3,
                    replica_groups, cc_in, cc_out, cc2_in, cc2_out, w1, b1, w2, b2)
        sWV = C.P.sem("wvout")
        dma(C.P, "sync", wvec_o, C.wvec, sWV, waits=[C.wvec_done])
        C.P.op("sync", lambda e: e.nop(), waits=[C.out_dma.w(), sWV.w()], inc=None)
        finish_build(nc, C, stack)
    return nc


def build_l1ffn_program():
    nc = bass.Bass("TRN2", target_bir_lowering=False)
    x3 = nc.dram_tensor("x3", [128, 32768], BF16, kind="ExternalInput").ap()
    wvec_i = nc.dram_tensor("wvec_i", [128, 8], F32, kind="ExternalInput").ap()
    wa = nc.dram_tensor("wa", [8, D, 4096], F32, kind="ExternalInput").ap()
    wb = nc.dram_tensor("wb", [8, 4096, D], F32, kind="ExternalInput").ap()
    out = nc.dram_tensor("out", [T, D], F32, kind="ExternalOutput").ap()
    with ExitStack() as stack:
        C = Ctx()
        C.nc = nc
        C.P = Prog()
        alloc_common(nc, stack.enter_context, C)
        emit_init(C)
        sWv = C.P.sem("wvin")
        C.wvec = C.small[:, 48:56]
        dma(C.P, "gpsimd", C.wvec, wvec_i, sWv)
        C.wvec_done = sWv.w()
        emit_l1ffn(C, x3, wa, wb, out)
        C.P.op("sync", lambda e: e.nop(), waits=[C.out_dma.w()], inc=None)
        finish_build(nc, C, stack)
    return nc


def build_fused_program(replica_groups):
    nc = bass.Bass("TRN2", target_bir_lowering=False)
    xin = nc.dram_tensor("xin", [T, D], F32, kind="ExternalInput").ap()
    ctx_in = nc.dram_tensor("ctx", [NCTX, DCTX], F32, kind="ExternalInput").ap()
    a_wq = nc.dram_tensor("a_wq", [D, D], F32, kind="ExternalInput").ap()
    a_wk = nc.dram_tensor("a_wk", [DCTX, D], F32, kind="ExternalInput").ap()
    a_wv = nc.dram_tensor("a_wv", [DCTX, D], F32, kind="ExternalInput").ap()
    a_wout = nc.dram_tensor("a_wout", [D, D], F32, kind="ExternalInput").ap()
    a_bout = nc.dram_tensor("a_bout", [D], F32, kind="ExternalInput").ap()
    f_wfc = nc.dram_tensor("f_wfc", [4096, D], F32, kind="ExternalInput").ap()
    f_lnw = nc.dram_tensor("f_lnw", [4096], F32, kind="ExternalInput").ap()
    f_lnb = nc.dram_tensor("f_lnb", [4096], F32, kind="ExternalInput").ap()
    f_wproj = nc.dram_tensor("f_wproj", [D, 4096], F32, kind="ExternalInput").ap()
    b_wq = nc.dram_tensor("b_wq", [D, D], F32, kind="ExternalInput").ap()
    b_bq = nc.dram_tensor("b_bq", [D], F32, kind="ExternalInput").ap()
    b_wk = nc.dram_tensor("b_wk", [D, D], F32, kind="ExternalInput").ap()
    b_bk = nc.dram_tensor("b_bk", [D], F32, kind="ExternalInput").ap()
    b_wv = nc.dram_tensor("b_wv", [D, D], F32, kind="ExternalInput").ap()
    b_bv = nc.dram_tensor("b_bv", [1, D], F32, kind="ExternalInput").ap()
    b_wo = nc.dram_tensor("b_wo", [D, D], F32, kind="ExternalInput").ap()
    b_bo = nc.dram_tensor("b_bo", [D], F32, kind="ExternalInput").ap()
    b_rf = nc.dram_tensor("b_rf", [64, 256], F32, kind="ExternalInput").ap()
    r_w1 = nc.dram_tensor("r_w1", [D, 256], F32, kind="ExternalInput").ap()
    r_b1 = nc.dram_tensor("r_b1", [256], F32, kind="ExternalInput").ap()
    r_w2 = nc.dram_tensor("r_w2", [256, 8], F32, kind="ExternalInput").ap()
    r_b2 = nc.dram_tensor("r_b2", [1, 8], F32, kind="ExternalInput").ap()
    m_wa = nc.dram_tensor("m_wa", [8, D, 4096], F32, kind="ExternalInput").ap()
    m_wb = nc.dram_tensor("m_wb", [8, 4096, D], F32, kind="ExternalInput").ap()
    kvK = nc.dram_tensor("kvK", [128, 8 * NCTX], BF16).ap()
    waScr = nc.dram_tensor("waScr", [4096, 8192], BF16).ap()
    wbScr = nc.dram_tensor("wbScr", [4096, 8192], BF16).ap()
    wkScr = nc.dram_tensor("wkScr", [1024, 1024], BF16).ap()
    wqScr = nc.dram_tensor("wqScr", [1024, 1024], BF16).ap()
    wvScr = nc.dram_tensor("wvScr", [1024, 1024], BF16).ap()
    woScr = nc.dram_tensor("woScr", [128, 8192], BF16).ap()
    kvV = nc.dram_tensor("kvV", [NCTX, D], BF16).ap()
    x1h = nc.dram_tensor("x1h", [128, 32768], BF16).ap()
    x2h = nc.dram_tensor("x2h", [128, 32768], BF16).ap()
    x3h = nc.dram_tensor("x3h", [128, 32768], BF16).ap()
    cc_in = nc.dram_tensor("cc_in", [128, 2048], F32)
    cc_out = nc.dram_tensor("cc_out", [128, 2048], F32)
    cc2_in = nc.dram_tensor("cc2_in", [128, 8], F32)
    cc2_out = nc.dram_tensor("cc2_out", [128, 8], F32)
    out = nc.dram_tensor("out", [T, D], F32, kind="ExternalOutput").ap()
    with ExitStack() as stack:
        C = Ctx()
        C.nc = nc
        C.P = Prog()
        alloc_common(nc, stack.enter_context, C)
        emit_init(C)
        pieces = []
        for wd, scrt in ((b_wk, wkScr), (b_wq, wqScr), (b_wv, wvScr)):
            for p in range(8):
                pieces.append((wd[:, 128 * p:128 * (p + 1)].rearrange("(k p2) o -> p2 k o", p2=128),
                               scrt[128 * p:128 * (p + 1), :], 8))
        for k in range(8):
            pieces.append((b_wo[128 * k:128 * (k + 1), :], woScr[:, 1024 * k:1024 * (k + 1)], 1))
        for z in range(8):
            for q in range(4):
                for k in range(8):
                    pieces.append((m_wa[z][128 * k:128 * (k + 1), 1024 * q:1024 * (q + 1)],
                                   waScr[(z * 4 + q) * 128:(z * 4 + q) * 128 + 128,
                                         1024 * k:1024 * (k + 1)], 1))
                for k in range(8):
                    pieces.append((m_wb[z][1024 * q + 128 * k:1024 * q + 128 * (k + 1), :],
                                   wbScr[(z * 4 + q) * 128:(z * 4 + q) * 128 + 128,
                                         1024 * k:1024 * (k + 1)], 1))
        register_precast(C, pieces)
        emit_l0attn(C, xin, ctx_in, a_wq, a_wk, a_wv, a_wout, a_bout, kvK, kvV, x1h)
        barrier(C, extra=[C.out_dma.w()])
        emit_l0ffn(C, x1h, f_wfc, f_lnw, f_lnb, f_wproj, x2h)
        pc_done = flush_precast(C)
        barrier(C, extra=[C.out_dma.w()] + pc_done)
        scrd = {"wk": [wkScr[128 * p:128 * (p + 1), :] for p in range(8)],
                "wq": [wqScr[128 * p:128 * (p + 1), :] for p in range(8)],
                "wv": [wvScr[128 * p:128 * (p + 1), :] for p in range(8)],
                "wo": woScr}
        emit_l1attn(C, x2h, b_wq, b_bq, b_wk, b_bk, b_wv, b_bv, b_wo, b_bo, b_rf,
                    x3h, replica_groups, cc_in, cc_out, cc2_in, cc2_out,
                    r_w1, r_b1, r_w2, r_b2, scr=scrd)
        barrier(C, extra=[C.out_dma.w()])
        waS = {}
        wbS = {}
        for z in range(8):
            for q in range(4):
                waS[z, q] = waScr[(z * 4 + q) * 128:(z * 4 + q) * 128 + 128, :]
                wbS[z, q] = wbScr[(z * 4 + q) * 128:(z * 4 + q) * 128 + 128, :]
        emit_l1ffn(C, x3h, waS, wbS, out)
        C.P.op("sync", lambda e: e.nop(), waits=[C.out_dma.w()], inc=None)
        finish_build(nc, C, stack)
    return nc


_prog_cache = {}


def _get_prog(name, builder):
    if name not in _prog_cache:
        _prog_cache[name] = builder()
    return _prog_cache[name]


def kernel(**inputs):
    x = np.ascontiguousarray(inputs["L0_attn_x"].reshape(32768, 1024))
    ctx = np.ascontiguousarray(inputs["L0_attn_context"].reshape(NCTX, DCTX))
    cores = list(range(NCORES))
    nc = _get_prog("fused", lambda: build_fused_program([[0, 1], [2, 3], [4, 5], [6, 7]]))
    maps = []
    for c in range(NCORES):
        maps.append({
            "xin": np.ascontiguousarray(x[T * c:T * (c + 1)]), "ctx": ctx,
            "a_wq": inputs["L0_attn_Wq"], "a_wk": inputs["L0_attn_Wk"],
            "a_wv": inputs["L0_attn_Wv"], "a_wout": inputs["L0_attn_Wout"],
            "a_bout": inputs["L0_attn_bout"],
            "f_wfc": inputs["L0_ffn_W_fc"], "f_lnw": inputs["L0_ffn_ln_w"],
            "f_lnb": inputs["L0_ffn_ln_b"], "f_wproj": inputs["L0_ffn_W_proj"],
            "b_wq": inputs["L1_attn_Wq"], "b_bq": inputs["L1_attn_bq"],
            "b_wk": inputs["L1_attn_Wk"], "b_bk": inputs["L1_attn_bk"],
            "b_wv": inputs["L1_attn_Wv"], "b_bv": inputs["L1_attn_bv"].reshape(1, -1),
            "b_wo": inputs["L1_attn_Wo"], "b_bo": inputs["L1_attn_bo"],
            "b_rf": inputs["L1_attn_random_features"],
            "r_w1": inputs["L1_ffn_W1"], "r_b1": inputs["L1_ffn_b1"],
            "r_w2": inputs["L1_ffn_W2"], "r_b2": inputs["L1_ffn_b2"].reshape(1, -1),
            "m_wa": inputs["L1_ffn_Wa"], "m_wb": inputs["L1_ffn_Wb"]})
    res = run_bass_kernel_spmd(nc, maps, core_ids=cores)
    outs = [res.results[c]["out"] for c in range(NCORES)]
    return np.concatenate(outs, axis=0).reshape(4, 8192, 1024)
